# version 14
# baseline (speedup 1.0000x reference)
import os
import numpy as np
import ml_dtypes
from contextlib import ExitStack

import concourse.bass as bass
import concourse.mybir as mybir
from concourse.bass_utils import run_bass_kernel_spmd
import concourse.bass_utils as _bu

if not getattr(_bu, "_ldw_patch", False):
    _orig_run_command = _bu.run_command

    def _run_command_ldw(cmd, **kw):
        if isinstance(cmd, list):
            cmd = ["--enable-ldw-opt=true" if c == "--enable-ldw-opt=false" else c
                   for c in cmd]
        return _orig_run_command(cmd, **kw)

    _bu.run_command = _run_command_ldw
    _bu._ldw_patch = True

F32 = mybir.dt.float32
BF16 = mybir.dt.bfloat16
U8 = mybir.dt.uint8
FP8 = mybir.dt.float8e4
AF = mybir.ActivationFunctionType
ALU = mybir.AluOpType
AX = mybir.AxisListType

S = 2048
E = 1024
INNER = 4096
NT = 16
ECH = 8
SCL = float(np.float32(1.0) / np.float32(np.sqrt(np.float32(32.0))))
MAGIC = 12582912.0
EPS = 1e-5
NEG = -1e30
NCORES = 8
KB = 1024

DEBUG = bool(int(os.environ.get("KDEBUG", "0")))
DEBUG_OUT = None

TERN_W = [("g", E, INNER), ("u", E, INNER), ("d", INNER, E)]


def tern_blocks(R, C):
    nh = max(1, C // 2048)
    cw = min(C, 2048)
    idx = 0
    for rb in range(R // 128):
        for h in range(nh):
            yield idx, rb, h, cw
            idx += 1


TERN_LIST = []
_bi = 0
for _nm, _R, _C in TERN_W:
    for _idx, _rb, _h, _cw in tern_blocks(_R, _C):
        TERN_LIST.append((_bi, _nm, _idx, _rb, _h, _cw))
        _bi += 1
NTERN = len(TERN_LIST)


class P:
    def __init__(self, passno, marks):
        self.passno = passno
        self.marks = marks
        self.counts = {}
        self.sems = {}

    def add_sem(self, nc, ctx, name):
        self.sems[name] = ctx.enter_context(nc.semaphore(name))
        self.counts[name] = 0

    def inc(self, instr, semname, n=1, mark=None):
        self.counts[semname] += n
        instr.then_inc(self.sems[semname], n)
        if mark is not None:
            self.marks[mark] = (semname, self.counts[semname])
        return self.counts[semname]

    def wait(self, eng, mark):
        if mark == "noop" or self.passno == 1:
            return
        semname, val = self.marks[mark]
        eng.wait_ge(self.sems[semname], val)


class Chain:
    def __init__(self, eng, p, sem, selfwait=True):
        self.e = eng
        self.p = p
        self.sem = sem
        self.selfwait = selfwait

    def wait(self, mark):
        self.p.wait(self.e, mark)

    def op(self, instr, mark=None):
        v = self.p.inc(instr, self.sem, 1, mark)
        if self.selfwait and self.p.passno == 2:
            self.e.wait_ge(self.p.sems[self.sem], v)
        return instr


def build_nc(passno, marks):
    nc = bass.Bass("TRN2", target_bir_lowering=False)
    p = P(passno, marks)

    x_d = nc.dram_tensor("x", [S, E], F32, kind="ExternalInput").ap()
    w_d = {}
    for nm in ["wq", "wk", "wv", "wo"]:
        w_d[nm] = nc.dram_tensor(nm, [E, E], F32, kind="ExternalInput").ap()
    b_d = {}
    for nm in ["bq", "bk", "bv", "bo"]:
        b_d[nm] = nc.dram_tensor(nm, [E], F32, kind="ExternalInput").ap()
    wf_d = {"g": nc.dram_tensor("wg", [E, INNER], F32, kind="ExternalInput").ap(),
            "u": nc.dram_tensor("wu", [E, INNER], F32, kind="ExternalInput").ap(),
            "d": nc.dram_tensor("wd", [INNER, E], F32, kind="ExternalInput").ap()}
    identb_d = nc.dram_tensor("identb", [128, 128], BF16, kind="ExternalInput").ap()
    y_d = nc.dram_tensor("y", [S, E], F32, kind="ExternalOutput").ap()

    skind = "ExternalOutput" if DEBUG else "Internal"
    tern_d = {"g": nc.dram_tensor("terng", [E, INNER], FP8, kind=skind).ap(),
              "u": nc.dram_tensor("ternu", [E, INNER], FP8, kind=skind).ap(),
              "d": nc.dram_tensor("ternd", [INNER, E], FP8, kind=skind).ap()}
    xprime_d = nc.dram_tensor("xprime", [S, E], F32, kind=skind).ap()
    if DEBUG:
        dbg_qf_d = nc.dram_tensor("dbg_qf", [128, ECH * S], BF16, kind="ExternalOutput").ap()
        dbg_kf_d = nc.dram_tensor("dbg_kf", [128, ECH * S], BF16, kind="ExternalOutput").ap()
        dbg_v_d = nc.dram_tensor("dbg_v", [128, NT * E], BF16, kind="ExternalOutput").ap()
        dbg_m8_d = nc.dram_tensor("dbg_m8", [128, NT * 32], F32, kind="ExternalOutput").ap()
        dbg_z_d = nc.dram_tensor("dbg_z", [128, NT * 8], F32, kind="ExternalOutput").ap()
        dbg_xq1_d = nc.dram_tensor("dbg_xq1", [128, E], BF16, kind="ExternalOutput").ap()
        dbg_htil_d = nc.dram_tensor("dbg_htil", [128, INNER], F32, kind="ExternalOutput").ap()
        dbg_xq2_d = nc.dram_tensor("dbg_xq2", [128, INNER], BF16, kind="ExternalOutput").ap()

    with ExitStack() as ctx:
        identb = ctx.enter_context(nc.sbuf_tensor("sb_ident", [128, 128], BF16))
        onesf = ctx.enter_context(nc.sbuf_tensor("onesf", [128, 1], F32))
        onesrow = ctx.enter_context(nc.sbuf_tensor("onesrow", [1, 128], F32))
        epscol = ctx.enter_context(nc.sbuf_tensor("epscol", [128, 1], F32))
        bcols = ctx.enter_context(nc.sbuf_tensor("bcols", [128, 2, ECH], F32))
        stats = ctx.enter_context(nc.sbuf_tensor("stats", [128, 64], F32))
        wsc = ctx.enter_context(nc.sbuf_tensor("wsc", [128, 16], F32))
        mv = ctx.enter_context(nc.sbuf_tensor("mv", [128, 16], F32))
        m8 = ctx.enter_context(nc.sbuf_tensor("m8", [128, 32], F32))
        zr = ctx.enter_context(nc.sbuf_tensor("zr", [128, 8], F32))
        osc4 = ctx.enter_context(nc.sbuf_tensor("osc4", [128, 4], F32))
        tA = ctx.enter_context(nc.sbuf_tensor("tA", [128, 4], F32))
        bnb = ctx.enter_context(nc.sbuf_tensor("bnb", [128, 12], F32))
        zc = ctx.enter_context(nc.sbuf_tensor("zc", [128, 2, 8], F32))
        if DEBUG:
            dbgm8 = ctx.enter_context(nc.sbuf_tensor("dbgm8", [128, NT, 32], F32))
            dbgz = ctx.enter_context(nc.sbuf_tensor("dbgz", [128, NT, 8], F32))

        AR = 203 * KB
        arena_t = ctx.enter_context(nc.sbuf_tensor("arena", [128, AR], U8))
        arena = arena_t[:]

        def carve(off_kb, kbytes, dt, inner_shape):
            off = int(off_kb * KB)
            ap = arena[:, off:off + int(kbytes * KB)].bitcast(dt)
            if inner_shape is None or len(inner_shape) == 1:
                return ap
            dims = " ".join(chr(ord('a') + i) for i in range(len(inner_shape)))
            kw = {chr(ord('a') + i): inner_shape[i] for i in range(len(inner_shape))}
            return ap.rearrange(f"p ({dims}) -> p {dims}", **kw)

        xn1T = carve(0, 32, BF16, [ECH, S])
        scA = carve(0, 8, F32, [S])
        scB = carve(8, 8, F32, [S])
        swork = carve(16, 8, F32, [S])
        e1b = carve(24, 4, BF16, [S])
        e2b = carve(28, 4, BF16, [S])
        qf = carve(32, 32, BF16, [ECH, S])
        kf = carve(64, 32, BF16, [ECH, S])
        vtok = carve(96, 32, BF16, [NT, E])
        wto = carve(128, 16, BF16, [ECH, E])
        wtt = carve(144, 16, BF16, [ECH, E])
        pbuf = carve(160, 4, BF16, [S])
        pbufB = carve(192, 4, BF16, [S])
        ptb = carve(164, 4, BF16, [NT, 128])
        aob = carve(168, 2, BF16, [E])
        aotb = carve(170, 2, BF16, [ECH, 128])
        xtile = carve(172, 8, F32, [2, E])
        xplus = carve(180, 4, F32, [E])
        xpst = carve(184, 8, F32, [2, E])
        bvbc = carve(192, 4, F32, [E])
        bobc = carve(196, 4, F32, [E])
        xnbf = carve(200, 2, BF16, [E])
        fwm = carve(96, 24, F32, [3, 2048])
        fwt = carve(176, 16, F32, [2, 2048])
        sgn2 = carve(128, 4, FP8, [2, 2048])
        msk2 = carve(132, 4, FP8, [2, 2048])
        terng = carve(0, 32, FP8, [ECH, INNER])
        ternu = carve(32, 32, FP8, [ECH, INNER])
        qstg = carve(64, 32, BF16, [8, 2048])
        ternd_r = carve(96, 32, FP8, [32, 1024])
        htil = carve(128, 16, F32, [INNER])
        xq2 = carve(144, 8, BF16, [INNER])
        cxa2 = carve(152, 8, F32, [2, E])
        xn2f = carve(160, 4, F32, [E])
        xq1 = carve(164, 2, BF16, [E])
        xq1T = carve(166, 2, BF16, [ECH, 128])
        silu2 = carve(168, 4, F32, [2, 512])
        xpin = carve(192, 4, F32, [2, 512])
        youts = carve(196, 4, F32, [2, 512])

        psS = ctx.enter_context(nc.psum_tensor("psS", [128, 2048], F32))
        psT = ctx.enter_context(nc.psum_tensor("psT", [128, 2048], BF16))
        psA = ctx.enter_context(nc.psum_tensor("psA", [128, 512], F32))
        psB = ctx.enter_context(nc.psum_tensor("psB", [128, 512], F32))

        for s_ in ["sPE", "sDV", "sSC", "sPO", "sC", "sX0", "sX1", "sF0", "sF1",
                   "sF2", "sG0", "sG1", "sW", "sTR", "sDS0", "sCX",
                   "sXI0", "sXI1", "sPW0", "sPW1", "sYW0", "sYW1", "sTW", "sTW0", "sTW1",
                   "sCX1"]:
            p.add_sem(nc, ctx, s_)
        block = ctx.enter_context(nc.Block())

        LASTT = TERN_LIST[-1]
        LAST_SGN = f"a2sgn_{LASTT[0]}"

        @block.sync
        def _(sync):
            i_ = sync.dma_start(out=identb[:], in_=identb_d)
            p.inc(i_, "sC", 16)
            i_ = sync.dma_start(out=bvbc[:], in_=b_d["bv"].unsqueeze(0).to_broadcast([128, E]))
            p.inc(i_, "sC", 16)
            i_ = sync.dma_start(out=bobc[:], in_=b_d["bo"].unsqueeze(0).to_broadcast([128, E]))
            p.inc(i_, "sC", 16)
            for j, nm in enumerate(["bq", "bk"]):
                for c in range(ECH):
                    i_ = sync.dma_start(out=bcols[:, j, c:c + 1],
                                        in_=b_d[nm][c * 128:(c + 1) * 128].unsqueeze(1))
                    p.inc(i_, "sC", 16)
            p.marks["a0_done"] = ("sC", p.counts["sC"])

            a1_iter = iter(range(NT))
            for k, (bi, nm, idx, rb, h, cw) in enumerate(TERN_LIST):
                if k % 4 == 0:
                    i = next(a1_iter, None)
                    if i is not None:
                        if i >= 2:
                            p.wait(sync, f"a1xnb_{i-2}")
                        i_ = sync.dma_start(out=xtile[:, i % 2, :],
                                            in_=x_d[i * 128:(i + 1) * 128, :])
                        p.inc(i_, f"sX{i % 2}", 16, f"a1x_{i}")
                if bi >= 3:
                    p.wait(sync, f"a2m_{bi-3}")
                i_ = sync.dma_start(out=fwm[:, bi % 3, 0:cw],
                                    in_=wf_d[nm][rb * 128:(rb + 1) * 128,
                                                 h * 2048:h * 2048 + cw])
                p.inc(i_, f"sF{bi % 3}", 16, f"a2mld_{bi}")

            for bi, nm, idx, rb, h, cw in TERN_LIST:
                if bi >= 2:
                    p.wait(sync, f"a2sgn_{bi-2}")
                i_ = sync.dma_start(out=fwt[:, bi % 2, 0:cw],
                                    in_=wf_d[nm][rb * 128:(rb + 1) * 128,
                                                 h * 2048:h * 2048 + cw])
                p.inc(i_, f"sG{bi % 2}", 16, f"a2tld_{bi}")
                if bi >= 1:
                    pbi, pnm, pidx, prb, ph, pcw = TERN_LIST[bi - 1]
                    p.wait(sync, f"a2tern_{pbi}")
                    i_ = sync.dma_start(out=tern_d[pnm][prb * 128:(prb + 1) * 128,
                                                        ph * 2048:ph * 2048 + pcw],
                                        in_=sgn2[:, pbi % 2, 0:pcw])
                    p.inc(i_, f"sTW{pbi % 2}", 16, f"a2out_{pbi}")
            p.wait(sync, f"a2tern_{LASTT[0]}")
            i_ = sync.dma_start(out=tern_d[LASTT[1]][LASTT[3] * 128:(LASTT[3] + 1) * 128,
                                                     LASTT[4] * 2048:LASTT[4] * 2048 + LASTT[5]],
                                in_=sgn2[:, LASTT[0] % 2, 0:LASTT[5]])
            p.inc(i_, f"sTW{LASTT[0] % 2}", 16, f"a2out_{LASTT[0]}")
            p.marks["tern_all0"] = ("sTW0", p.counts["sTW0"])
            p.marks["tern_all1"] = ("sTW1", p.counts["sTW1"])

            for i in range(NT):
                if i < 2:
                    p.wait(sync, f"a1xnb_{14+i}")
                    p.wait(sync, "tern_all0")
                    p.wait(sync, "tern_all1")
                else:
                    p.wait(sync, f"xplus_{i-2}")
                i_ = sync.dma_start(out=xtile[:, i % 2, :], in_=x_d[i * 128:(i + 1) * 128, :])
                p.inc(i_, f"sX{i % 2}", 16, f"bx_{i}")
                if i >= 1:
                    p.wait(sync, f"xp_{i-1}_1")
                    i_ = sync.dma_start(out=xprime_d[(i - 1) * 128:i * 128, :],
                                        in_=xpst[:, (i - 1) % 2, :])
                    p.inc(i_, f"sPW{(i - 1) % 2}", 16, f"xpw_{i-1}")
            p.wait(sync, "xp_15_1")
            i_ = sync.dma_start(out=xprime_d[15 * 128:16 * 128, :], in_=xpst[:, 1, :])
            p.inc(i_, "sPW1", 16, "xpw_15")

            p.wait(sync, "exp_15")
            p.wait(sync, "P_15")
            p.wait(sync, "sc_mm_15")
            i_ = sync.dma_start(out=terng[:, :, :],
                               in_=tern_d["g"].rearrange("(c p) i -> p c i", p=128))
            p.inc(i_, "sTR", 16)
            i_ = sync.dma_start(out=ternu[:, :, :],
                               in_=tern_d["u"].rearrange("(c p) i -> p c i", p=128))
            p.inc(i_, "sTR", 16, "ld_ternu")
            p.wait(sync, "av_mm_15_1")
            i_ = sync.dma_start(out=ternd_r[:, :, :],
                                in_=tern_d["d"].rearrange("(ic p) e -> p ic e", p=128))
            p.inc(i_, "sDS0", 16, "ld_ternd")

            for g in range(4):
                for j in range(4):
                    t = 4 * g + j
                    p.wait(sync, f"xpw_{t}")
                    if t == 0:
                        p.wait(sync, "xplus_15")
                    if t >= 2:
                        p.wait(sync, f"c1ln_{t-2}")
                    i_ = sync.dma_start(out=cxa2[:, t % 2, :],
                                        in_=xprime_d[t * 128:(t + 1) * 128, :])
                    p.inc(i_, "sCX" if t % 2 == 0 else "sCX1", 16, f"cxp_{t}")
            if DEBUG:
                p.wait(sync, "dbg_ready")
                for ap_, dd in [(qf, dbg_qf_d), (kf, dbg_kf_d), (vtok, dbg_v_d)]:
                    i_ = sync.dma_start(out=dd, in_=ap_[:, :, :])
                    p.inc(i_, "sC", 16)
                i_ = sync.dma_start(out=dbg_m8_d, in_=dbgm8[:, :, :])
                p.inc(i_, "sC", 16)
                i_ = sync.dma_start(out=dbg_z_d, in_=dbgz[:, :, :])
                p.inc(i_, "sC", 16)
                p.marks["dbg_st"] = ("sC", p.counts["sC"])
                p.wait(sync, "dbg_st")
            for fm in ["fin_sPW0", "fin_sPW1", "fin_sYW0", "fin_sYW1"]:
                p.wait(sync, fm)

        @block.tensor
        def _(tensor):
            T = Chain(tensor, p, "sPE", selfwait=False)
            T.wait("a0_done")
            for i in range(NT):
                T.wait(f"a1xnb_{i}")
                if i >= 1:
                    T.wait(f"a1xnT_ev_{i-1}")
                for c in range(ECH):
                    i_ = tensor.transpose(psT[:, c * 128:(c + 1) * 128],
                                          xnbf[:, c * 128:(c + 1) * 128], identb[:])
                T.op(i_, f"a1xnT_{i}")
            T.wait("a2r_done")
            for wi in range(3):
                i_ = tensor.matmul(psA[0:1, wi:wi + 1], tA[:, wi:wi + 1], onesf[:],
                                   start=True, stop=True)
            T.op(i_, "a2tot_mm")
            T.wait("a2tot_ev")
            i_ = tensor.matmul(psB[:, 0:4], onesrow[:], tA[0:1, 0:4], start=True, stop=True)
            T.op(i_, "a2bc_mm")
            for nm in ["q", "k"]:
                T.wait(f"wld_{nm}")
                T.wait("a1xnT_ev_15")
                dst = qf if nm == "q" else kf
                for oc in range(ECH):
                    if oc >= 1:
                        T.wait(f"qk_ev_{nm}_{oc-1}")
                    elif nm == "k":
                        T.wait("qk_ev_q_7")
                    for sc in range(4):
                        for c in range(ECH):
                            i_ = tensor.matmul(psS[:, sc * 512:(sc + 1) * 512],
                                               wtt[:, c, oc * 128:(oc + 1) * 128],
                                               xn1T[:, c, sc * 512:(sc + 1) * 512],
                                               start=(c == 0), stop=(c == ECH - 1))
                    T.op(i_, f"qk_mm_{nm}_{oc}")
            T.wait("wld_v")
            T.wait("qk_ev_k_7")
            for st in range(NT):
                if st >= 1:
                    T.wait(f"v_ev_{st-1}")
                for oh in range(2):
                    for c in range(ECH):
                        i_ = tensor.matmul(psS[:, oh * 512:(oh + 1) * 512],
                                           xn1T[:, c, st * 128:(st + 1) * 128],
                                           wtt[:, c, oh * 512:(oh + 1) * 512],
                                           start=(c == 0), stop=(c == ECH - 1))
                T.op(i_, f"v_mm_{st}")
            def pe_scores(i):
                for sc in range(4):
                    for c in range(ECH):
                        i_ = tensor.matmul(psS[:, sc * 512:(sc + 1) * 512],
                                           qf[:, c, i * 128:(i + 1) * 128],
                                           kf[:, c, sc * 512:(sc + 1) * 512],
                                           start=(c == 0), stop=(c == ECH - 1))
                    T.op(i_, f"sc_mm_{i}_{sc}")
                p.marks[f"sc_mm_{i}"] = ("sPE", p.counts["sPE"])

            T.wait("v_ev_15")
            pe_scores(0)
            T.wait("sc_ev_0")
            pe_scores(1)
            for t in range(NT):
                T.wait(f"P_{t}")
                T.wait(f"pt_ev_{t-1}" if t >= 1 else "a1xnT_ev_15")
                pb = pbuf if t % 2 == 0 else pbufB
                for c in range(NT):
                    i_ = tensor.transpose(psT[:, c * 128:(c + 1) * 128],
                                          pb[:, c * 128:(c + 1) * 128], identb[:])
                T.op(i_, f"pt_mm_{t}")
                T.wait(f"pt_ev_{t}")
                for oh in range(2):
                    if oh == 0:
                        T.wait(f"xp_{t-1}_0" if t >= 1 else "a2tot_ev")
                    else:
                        T.wait(f"xp_{t-1}_1" if t >= 1 else "a2sc_done")
                    avdst = psA if oh == 0 else psB
                    for kt in range(NT):
                        i_ = tensor.matmul(avdst[:, :], ptb[:, kt, :],
                                           vtok[:, kt, oh * 512:(oh + 1) * 512],
                                           start=(kt == 0), stop=(kt == NT - 1))
                    T.op(i_, f"av_mm_{t}_{oh}")
                if t + 2 < NT:
                    T.wait(f"sc_ev_{t+1}")
                    pe_scores(t + 2)
                T.wait(f"av_ev_{t}_1")
                for c in range(ECH):
                    i_ = tensor.transpose(psT[:, c * 128:(c + 1) * 128],
                                          aob[:, c * 128:(c + 1) * 128], identb[:])
                T.op(i_, f"aot_mm_{t}")
                T.wait(f"aot_ev_{t}")
                if t == 0:
                    T.wait("wld_o")
                for oh in range(2):
                    if oh == 0:
                        T.wait(f"av_ev_{t}_0")
                    odst = psA if oh == 0 else psB
                    for c in range(ECH):
                        i_ = tensor.matmul(odst[:, :], aotb[:, c, :],
                                           wto[:, c, oh * 512:(oh + 1) * 512],
                                           start=(c == 0), stop=(c == ECH - 1))
                    T.op(i_, f"o_mm_{t}_{oh}")
            def pe_xq1T(t):
                T.wait(f"xq1_{t}")
                T.wait(f"xq2T_ev_{t-2}_1" if t >= 2 else
                       ("xq1T_ev_0" if t == 1 else "aot_ev_15"))
                for c in range(ECH):
                    i_ = tensor.transpose(psT[:, c * 128:(c + 1) * 128],
                                          xq1[:, c * 128:(c + 1) * 128], identb[:])
                T.op(i_, f"xq1T_mm_{t}")

            def pe_xq2T(t):
                T.wait(f"xq2_{t}")
                T.wait(f"xq1T_ev_{t+1}" if t + 1 < NT else "xq2T_ev_14_1")
                for half in range(2):
                    if half == 1:
                        T.wait(f"xq2T_ev_{t}_0")
                    for c in range(16):
                        i_ = tensor.transpose(
                            psT[:, c * 128:(c + 1) * 128],
                            xq2[:, half * 2048 + c * 128:half * 2048 + (c + 1) * 128],
                            identb[:])
                    T.op(i_, f"xq2T_mm_{t}_{half}")

            pe_xq1T(0)
            for g in range(4):
                for j in range(4):
                    t = 4 * g + j
                    T.wait(f"xq1T_ev_{t}")
                    if t == 0:
                        T.wait("ld_ternu")
                        T.wait("sc_ev_15")
                    for ib in range(8):
                        gbi = t * 8 + ib
                        T.wait(f"gu_ev_{gbi-3}" if gbi >= 3 else "noop")
                        if g >= 1 and j == 0 and ib == 0:
                            for uu in range(4):
                                T.wait(f"c3u_{8*(g-1)+4+uu}")
                        r3 = gbi % 3
                        gps = (psA[:, :] if r3 == 0 else
                               psS[:, 0:512] if r3 == 1 else psS[:, 1024:1536])
                        ups = (psB[:, :] if r3 == 0 else
                               psS[:, 512:1024] if r3 == 1 else psS[:, 1536:2048])
                        for c in range(ECH):
                            i_ = tensor.matmul(gps, xq1T[:, c, :],
                                               terng[:, c, ib * 512:(ib + 1) * 512],
                                               start=(c == 0), stop=(c == ECH - 1))
                        T.op(i_, f"g_mm_{gbi}")
                        for c in range(ECH):
                            i_ = tensor.matmul(ups, xq1T[:, c, :],
                                               ternu[:, c, ib * 512:(ib + 1) * 512],
                                               start=(c == 0), stop=(c == ECH - 1))
                        T.op(i_, f"u_mm_{gbi}")
                        if ib == 2 and t >= 1 and (t - 1) % 4 != 3:
                            pe_xq2T(t - 1)
                    if t + 1 < NT:
                        pe_xq1T(t + 1)
                    if j == 3:
                        pe_xq2T(t)
                        base = 32 * g
                        lr1 = max(b for b in range(base, base + 32) if b % 3 == 1)
                        lr2 = max(b for b in range(base, base + 32) if b % 3 == 2)
                        for h in range(2):
                            for j2 in range(4):
                                if j2 == 0 and h == 0 and g == 0:
                                    T.wait("ld_ternd")
                                if h == 1:
                                    T.wait(f"c3_{4*g+j2}_0")
                                elif g >= 1:
                                    T.wait(f"c3_{4*(g-1)+j2}_1")
                                if h == 0:
                                    T.wait(f"gu_ev_{lr1 if j2 < 2 else lr2}")
                                    T.wait(f"xq2T_ev_{4*g+j2}_0")
                                    T.wait(f"xq2T_ev_{4*g+j2}_1")
                                for ic in range(32):
                                    i_ = tensor.matmul(
                                        psS[:, j2 * 512:(j2 + 1) * 512],
                                        qstg[:, j2 * 2 + ic // 16,
                                             (ic % 16) * 128:(ic % 16 + 1) * 128],
                                        ternd_r[:, ic, h * 512:(h + 1) * 512],
                                        start=(ic == 0), stop=(ic == 31))
                                T.op(i_, f"c2mmj_{g}_{h}_{j2}")
                            p.marks[f"c2mm_{g}_{h}"] = ("sPE", p.counts["sPE"])

        @block.vector
        def _(vector):
            V = Chain(vector, p, "sDV")
            V.op(vector.memset(onesf[:], 1.0))
            V.op(vector.memset(onesrow[:], 1.0))
            V.op(vector.memset(epscol[:], EPS), "consts")
            if DEBUG:
                V.op(vector.memset(dbgm8[:, :, :], 0.0))
                V.op(vector.memset(dbgz[:, :, :], 0.0))

            def ln_stats(src):
                V.op(vector.bn_stats(bnb[:, 0:6], src[:, 0:512]))
                V.op(vector.bn_stats(bnb[:, 6:12], src[:, 512:1024]))
                V.op(vector.bn_aggr(mv[:, 12:14], bnb[:, 0:12]))

            base = {"g": 0, "u": 16, "d": 32}
            a1_iter = iter(range(NT))
            for k, (bi, nm, idx, rb, h, cw) in enumerate(TERN_LIST):
                if k % 4 == 0:
                    i = next(a1_iter, None)
                    if i is not None:
                        V.wait(f"a1x_{i}")
                        ln_stats(xtile[:, i % 2, :])
                        p.marks[f"a1bn_{i}"] = ("sDV", p.counts["sDV"])
                        V.wait(f"a1rstd_{i}")
                        V.op(vector.tensor_scalar(out=xnbf[:], in0=xtile[:, i % 2, :],
                                                  scalar1=mv[:, 12:13], scalar2=mv[:, 13:14],
                                                  op0=ALU.subtract, op1=ALU.mult),
                             f"a1xnb_{i}")
                        V.wait(f"a1xnT_{i}")
                        V.op(vector.tensor_copy(xn1T[:, :, i * 128:(i + 1) * 128],
                                                psT[:, 0:1024].rearrange("p (c t) -> p c t", c=8)),
                             f"a1xnT_ev_{i}")
                V.wait(f"a2mld_{bi}")
                V.op(vector.reduce_sum(stats[:, base[nm] + idx:base[nm] + idx + 1],
                                       fwm[:, bi % 3, 0:cw], axis=AX.X,
                                       apply_absolute_value=True), f"a2m_{bi}")
            for wi, (nm, R, C) in enumerate(TERN_W):
                nidx = (R // 128) * max(1, C // 2048)
                V.op(vector.reduce_sum(tA[:, wi:wi + 1], stats[:, base[nm]:base[nm] + nidx],
                                       axis=AX.X))
            V.op(vector.tensor_copy(tA[:, 3:4], tA[:, 0:1]), "a2r_done")
            V.wait("a2tot_mm")
            V.op(vector.tensor_copy(tA[0:1, 0:3], psA[0:1, 0:3]), "a2tot_ev")
            V.wait("a2bc_mm")
            NW = float(INNER * E)
            V.op(vector.tensor_scalar(out=wsc[:, 0:3], in0=psB[:, 0:3],
                                      scalar1=1.0 / NW, scalar2=EPS,
                                      op0=ALU.mult, op1=ALU.max))
            V.op(vector.tensor_scalar_mul(wsc[:, 3:6], wsc[:, 0:3], 0.5))
            V.op(vector.tensor_scalar_mul(wsc[:, 9:12], wsc[:, 3:6], -1.0))
            V.op(vector.tensor_scalar_mul(wsc[:, 12:13], wsc[:, 0:1], 0.5))
            V.op(vector.reciprocal(wsc[:, 6:9], wsc[:, 0:3]), "a2sc_done")

            qkv_ev = ([("q", oc) for oc in range(ECH)] + [("k", oc) for oc in range(ECH)]
                      + [("v", st) for st in range(NT)])
            ev_iter = iter(qkv_ev)
            for bi, nm, idx, rb, h, cw in TERN_LIST:
                V.wait(f"a2sgn_{bi}")
                V.op(vector.tensor_add(sgn2[:, bi % 2, 0:cw], sgn2[:, bi % 2, 0:cw],
                                       msk2[:, bi % 2, 0:cw]), f"a2tern_{bi}")
                if bi % 2 == 1:
                    ev = next(ev_iter, None)
                    if ev is not None:
                        _emit_qkv_ev(V, vector, p, ev, qf, kf, vtok, bcols, bvbc, psS)
            for ev in ev_iter:
                _emit_qkv_ev(V, vector, p, ev, qf, kf, vtok, bcols, bvbc, psS)

            def emit_topk(t):
                sb = scA if t % 2 == 0 else scB
                V.wait(f"sc_ev_{t}")
                if t >= 1:
                    V.wait(f"exp_{t-1}")
                src = sb
                for r in range(4):
                    V.op(vector.max(out=m8[:, 8 * r:8 * r + 8], in_=src[:, :]))
                    if r == 0:
                        V.op(vector.tensor_scalar_mul(mv[:, 14:15], m8[:, 0:1], -1.0),
                             f"r1_{t}")
                    V.op(vector.match_replace(out=swork[:, :],
                                              in_to_replace=m8[:, 8 * r:8 * r + 8],
                                              in_values=src[:, :], imm_value=NEG))
                    src = swork
                    if t >= 1 and r == 1:
                        V.wait(f"av_mm_{t-1}_0")
                        V.op(vector.tensor_scalar_mul(aob[:, 0:512], psA[:, :],
                                                      zr[:, 1 + 2 * ((t - 1) % 2):2 + 2 * ((t - 1) % 2)]),
                             f"av_ev_{t-1}_0")
                    if t >= 1 and r == 2:
                        V.wait(f"av_mm_{t-1}_1")
                        V.op(vector.tensor_scalar_mul(aob[:, 512:1024], psB[:, :],
                                                      zr[:, 1 + 2 * ((t - 1) % 2):2 + 2 * ((t - 1) % 2)]),
                             f"av_ev_{t-1}_1")
                V.op(vector.tensor_copy(tA[:, 3:4], m8[:, 31:32]), f"tk_{t}")
                if DEBUG:
                    V.op(vector.tensor_copy(dbgm8[:, t, :], m8[:, :]))

            def emit_P(t):
                pb = pbuf if t % 2 == 0 else pbufB
                zc = 2 * (t % 2)
                V.wait(f"exp_{t}")
                V.op(vector.scalar_tensor_tensor(out=pb[:, :], in0=e1b[:, :], scalar=1.0,
                                                 in1=e2b[:, :], op0=ALU.mult,
                                                 op1=ALU.subtract, accum_out=zr[:, zc:zc + 1]),
                     f"P_{t}")
                V.op(vector.reciprocal(zr[:, zc + 1:zc + 2], zr[:, zc:zc + 1]))
                if DEBUG:
                    V.op(vector.tensor_copy(dbgz[:, t, 0:2], zr[:, zc:zc + 2]))

            def emit_xp(t):
                for oh in range(2):
                    V.wait(f"o_mm_{t}_{oh}")
                    V.wait(f"xplus_{t}")
                    if t == 0 and oh == 0:
                        V.wait("tern_all0")
                        V.wait("tern_all1")
                    if t >= 2 and oh == 0:
                        V.wait(f"xpw_{t-2}")
                    V.op(vector.scalar_tensor_tensor(
                        out=xpst[:, t % 2, oh * 512:(oh + 1) * 512],
                        in0=(psA if oh == 0 else psB)[:, :], scalar=1.0,
                        in1=xplus[:, oh * 512:(oh + 1) * 512],
                        op0=ALU.mult, op1=ALU.add), f"xp_{t}_{oh}")

            emit_topk(0)
            emit_P(0)
            for t in range(NT - 1):
                emit_topk(t + 1)
                emit_P(t + 1)
                emit_xp(t)
            t = NT - 1
            V.wait(f"av_mm_{t}_0")
            V.op(vector.tensor_scalar_mul(aob[:, 0:512], psA[:, :],
                                          zr[:, 1 + 2 * (t % 2):2 + 2 * (t % 2)]),
                 f"av_ev_{t}_0")
            V.wait(f"av_mm_{t}_1")
            V.op(vector.tensor_scalar_mul(aob[:, 512:1024], psB[:, :],
                                          zr[:, 1 + 2 * (t % 2):2 + 2 * (t % 2)]),
                 f"av_ev_{t}_1")
            emit_xp(NT - 1)

            def c_head(t):
                sl = t % 2
                V.wait(f"cxp_{t}")
                if t == 0:
                    V.wait("xpw_14")
                    V.wait("xpw_15")
                ln_stats(cxa2[:, sl, :])
                p.marks[f"c1bn_{t}"] = ("sDV", p.counts["sDV"])
                V.wait(f"c1rstd_{t}")
                V.op(vector.tensor_scalar(out=xn2f[:], in0=cxa2[:, sl, :],
                                          scalar1=mv[:, 12:13], scalar2=mv[:, 13:14],
                                          op0=ALU.subtract, op1=ALU.mult), f"c1ln_{t}")
                V.op(vector.reduce_max(zc[:, sl, 0:1], xn2f[:], axis=AX.X,
                                       apply_absolute_value=True))
                V.op(vector.tensor_scalar_max(zc[:, sl, 0:1], zc[:, sl, 0:1], EPS))
                V.op(vector.reciprocal(zc[:, sl, 1:2], zc[:, sl, 0:1]))
                V.op(vector.tensor_scalar_mul(zc[:, sl, 0:1], zc[:, sl, 1:2], 127.0))
                V.op(vector.reciprocal(zc[:, sl, 1:2], zc[:, sl, 0:1]))
                V.op(vector.tensor_mul(zc[:, sl, 2:3], zc[:, sl, 1:2], wsc[:, 12:13]),
                     f"cg_{t}")
                V.op(vector.tensor_scalar(out=xn2f[:], in0=xn2f[:],
                                          scalar1=zc[:, sl, 0:1], scalar2=MAGIC,
                                          op0=ALU.mult, op1=ALU.add))
                if t >= 1:
                    V.wait(f"xq1T_ev_{t-1}")
                if DEBUG and t == 1:
                    V.wait("dbg_xq1w")
                V.op(vector.tensor_scalar(out=xq1[:], in0=xn2f[:], scalar1=MAGIC,
                                          scalar2=None, op0=ALU.subtract), f"xq1_{t}")
                if DEBUG and t == 0:
                    p.marks["dbg_xq1r"] = ("sDV", p.counts["sDV"])

            def c_body(t):
                for ib in range(8):
                    gbi = t * 8 + ib
                    r3 = gbi % 3
                    ups = (psB[:, :] if r3 == 0 else
                           psS[:, 512:1024] if r3 == 1 else psS[:, 1536:2048])
                    V.wait(f"silu_{gbi}")
                    V.wait(f"u_mm_{gbi}")
                    V.op(vector.scalar_tensor_tensor(
                        out=htil[:, ib * 512:(ib + 1) * 512], in0=ups, scalar=1.0,
                        in1=silu2[:, gbi % 2, :], op0=ALU.mult, op1=ALU.mult),
                        f"gu_ev_{gbi}")
                    V.op(vector.reduce_max(stats[:, ib:ib + 1],
                                           htil[:, ib * 512:(ib + 1) * 512],
                                           axis=AX.X, apply_absolute_value=True))
                p.marks[f"htil_{t}"] = ("sDV", p.counts["sDV"])

            def c_tail(t, j):
                sl = t % 2
                V.op(vector.reduce_max(zc[:, sl, 3:4], stats[:, 0:8], axis=AX.X),
                     f"hmax_{t}")
                if DEBUG and t == 0:
                    V.wait("dbg_htw")
                V.op(vector.tensor_mul(tA[:, 0:1], zc[:, sl, 0:1], wsc[:, 7:8]))
                V.op(vector.tensor_scalar_mul(tA[:, 0:1], tA[:, 0:1], EPS * 2.0))
                V.op(vector.tensor_max(zc[:, sl, 3:4], zc[:, sl, 3:4], tA[:, 0:1]))
                V.op(vector.reciprocal(zc[:, sl, 4:5], zc[:, sl, 3:4]))
                V.op(vector.tensor_scalar_mul(zc[:, sl, 4:5], zc[:, sl, 4:5], 127.0))
                V.op(vector.tensor_mul(tA[:, 1:2], zc[:, sl, 3:4], zc[:, sl, 1:2]))
                V.op(vector.tensor_mul(tA[:, 1:2], tA[:, 1:2], wsc[:, 1:2]))
                V.op(vector.tensor_mul(tA[:, 1:2], tA[:, 1:2], wsc[:, 2:3]))
                V.op(vector.tensor_scalar_mul(osc4[:, j:j + 1], tA[:, 1:2],
                                              1.0 / (127.0 * 4.0)), f"osc_{t}")
                if DEBUG:
                    V.op(vector.tensor_copy(dbgz[:, t, 2:7], zc[:, sl, 0:5]))
                    V.op(vector.tensor_copy(dbgz[:, t, 7:8], osc4[:, j:j + 1]))
                V.op(vector.tensor_scalar(out=htil[:, :], in0=htil[:, :],
                                          scalar1=zc[:, sl, 4:5], scalar2=MAGIC,
                                          op0=ALU.mult, op1=ALU.add))
                if t >= 1:
                    V.wait(f"xq2T_mm_{t-1}_1")
                    if DEBUG and t == 1:
                        V.wait("dbg_xq2w")
                V.op(vector.tensor_scalar(out=xq2[:], in0=htil[:, :], scalar1=MAGIC,
                                          scalar2=None, op0=ALU.subtract), f"xq2_{t}")
                if DEBUG and t == 0:
                    p.marks["dbg_xq2r"] = ("sDV", p.counts["sDV"])

            c_head(0)
            for t in range(NT):
                g, j = t // 4, t % 4
                if t + 1 < NT:
                    c_head(t + 1)
                c_body(t)
                c_tail(t, j)
                if j == 3:
                    _emit_c3(V, vector, p, g, psS, osc4, youts, xpin, DEBUG)
            if DEBUG:
                V.op(vector.tensor_copy(tA[:, 0:1], tA[:, 0:1]), "dbg_ready")

        @block.scalar
        def _(scalar):
            C_ = Chain(scalar, p, "sSC")
            for i in range(NT):
                C_.wait(f"a1bn_{i}")
                C_.op(scalar.activation(out=mv[:, 15:16], in_=mv[:, 13:14], func=AF.Ln,
                                        bias=epscol[:], scale=1.0))
                C_.op(scalar.activation(out=mv[:, 13:14], in_=mv[:, 15:16], func=AF.Exp,
                                        scale=-0.5), f"a1rstd_{i}")
            for bi, nm, idx, rb, h, cw in TERN_LIST:
                wi = {"g": 0, "u": 1, "d": 2}[nm]
                C_.wait(f"a2tld_{bi}")
                if bi == 0:
                    C_.wait("a2sc_done")
                if bi >= 2:
                    C_.wait(f"a2out_{bi-2}")
                C_.op(scalar.activation(out=sgn2[:, bi % 2, 0:cw], in_=fwt[:, bi % 2, 0:cw],
                                        func=AF.Sign, bias=wsc[:, 9 + wi:10 + wi]))
                C_.op(scalar.activation(out=msk2[:, bi % 2, 0:cw], in_=fwt[:, bi % 2, 0:cw],
                                        func=AF.Sign, bias=wsc[:, 3 + wi:4 + wi]),
                      f"a2sgn_{bi}")
            def s_scev(i):
                nb = scA if i % 2 == 0 else scB
                for sc4 in range(4):
                    C_.wait(f"sc_mm_{i}_{sc4}")
                    C_.op(scalar.copy(nb[:, sc4 * 512:(sc4 + 1) * 512],
                                      psS[:, sc4 * 512:(sc4 + 1) * 512]),
                          f"sc_ev_{i}" if sc4 == 3 else None)

            C_.wait("sc_mm_0")
            C_.op(scalar.copy(scA[:, :], psS[:, :]), "sc_ev_0")
            C_.wait("r1_0")
            C_.op(scalar.activation(out=e1b[:, :], in_=scA[:, :], func=AF.Exp,
                                    bias=mv[:, 14:15], scale=1.0))
            s_scev(1)
            C_.wait("tk_0")
            C_.op(scalar.activation(out=e2b[:, :], in_=swork[:, :], func=AF.Exp,
                                    bias=mv[:, 14:15], scale=1.0), "exp_0")
            for t in range(NT):
                C_.wait(f"pt_mm_{t}")
                C_.op(scalar.copy(ptb[:, :, :],
                                  psT[:, :].rearrange("p (c t) -> p c t", t=128)),
                      f"pt_ev_{t}")
                if t + 1 < NT:
                    nb = scB if t % 2 == 0 else scA
                    C_.wait(f"r1_{t+1}")
                    C_.op(scalar.activation(out=e1b[:, :], in_=nb[:, :], func=AF.Exp,
                                            bias=mv[:, 14:15], scale=1.0))
                if t + 2 < NT:
                    s_scev(t + 2)
                C_.wait(f"aot_mm_{t}")
                C_.op(scalar.copy(aotb[:, :, :],
                                  psT[:, 0:1024].rearrange("p (c t) -> p c t", c=8)),
                      f"aot_ev_{t}")
                if t + 1 < NT:
                    C_.wait(f"tk_{t+1}")
                    C_.op(scalar.activation(out=e2b[:, :], in_=swork[:, :], func=AF.Exp,
                                            bias=mv[:, 14:15], scale=1.0), f"exp_{t+1}")
            def s_head(t):
                C_.wait(f"c1bn_{t}")
                C_.op(scalar.activation(out=mv[:, 15:16], in_=mv[:, 13:14], func=AF.Ln,
                                        bias=epscol[:], scale=1.0))
                C_.op(scalar.activation(out=mv[:, 13:14], in_=mv[:, 15:16], func=AF.Exp,
                                        scale=-0.5), f"c1rstd_{t}")

            def s_xq1T(t):
                C_.wait(f"xq1T_mm_{t}")
                C_.op(scalar.copy(xq1T[:, :, :],
                                  psT[:, 0:1024].rearrange("p (c t) -> p c t", c=8)),
                      f"xq1T_ev_{t}")

            def s_xq2T(t):
                for half in range(2):
                    C_.wait(f"xq2T_mm_{t}_{half}")
                    if t >= 4:
                        C_.wait(f"c2mm_{t//4-1}_1")
                    C_.op(scalar.copy(qstg[:, (t % 4) * 2 + half, :], psT[:, :]),
                          f"xq2T_ev_{t}_{half}")

            s_head(0)
            s_xq1T(0)
            for g in range(4):
                for j in range(4):
                    t = 4 * g + j
                    if t >= 1 and (t - 1) % 4 != 3:
                        s_xq2T(t - 1)
                    if t + 1 < NT:
                        s_head(t + 1)
                    C_.wait(f"cg_{t}")
                    for ib in range(8):
                        gbi = t * 8 + ib
                        r3 = gbi % 3
                        gps = (psA[:, :] if r3 == 0 else
                               psS[:, 0:512] if r3 == 1 else psS[:, 1024:1536])
                        C_.wait(f"g_mm_{gbi}")
                        if gbi >= 2:
                            C_.wait(f"gu_ev_{gbi-2}")
                        C_.op(scalar.activation(out=silu2[:, gbi % 2, :], in_=gps,
                                                func=AF.Silu, scale=zc[:, t % 2, 2:3]),
                              f"silu_{gbi}")
                    if t + 1 < NT:
                        s_xq1T(t + 1)
                    if j == 3:
                        s_xq2T(t)

        @block.gpsimd
        def _(gpsimd):
            G = Chain(gpsimd, p, "sPO")
            for nm, gate in [("q", None), ("k", "qk_mm_q_7"), ("v", "qk_mm_k_7"),
                             ("o", "tern_all0")]:
                if gate:
                    p.wait(gpsimd, gate)
                    if gate == "tern_all0":
                        p.wait(gpsimd, "tern_all1")
                dst = wto if nm == "o" else wtt
                for ec in range(ECH):
                    i_ = gpsimd.dma_start(out=dst[:, ec, :],
                                          in_=w_d["w" + nm][ec * 128:(ec + 1) * 128, :])
                    p.inc(i_, "sW", 16)
                p.marks[f"wld_{nm}"] = ("sW", p.counts["sW"])
            G.wait("ld_bobc") if False else p.wait(gpsimd, "a0_done")
            for i in range(NT):
                G.wait(f"bx_{i}")
                if i >= 1:
                    G.wait(f"xp_{i-1}_1")
                G.op(gpsimd.tensor_add(xplus[:], xtile[:, i % 2, :], bobc[:]), f"xplus_{i}")
            steps_all = [(g_ * 8 + h_ * 4 + j_, 4 * g_ + j_, h_)
                         for g_ in range(4) for h_ in range(2) for j_ in range(4)]
            for g in range(4):
                if DEBUG and g == 0:
                    for wm, dd, ap_ in [("dbg_xq1r", dbg_xq1_d, xq1[:]),
                                        ("hmax_0", dbg_htil_d, htil[:, :]),
                                        ("dbg_xq2r", dbg_xq2_d, xq2[:])]:
                        p.wait(gpsimd, wm)
                        i_ = gpsimd.dma_start(out=dd, in_=ap_)
                        v_ = p.inc(i_, "sTW", 16,
                                   {"dbg_xq1r": "dbg_xq1w", "hmax_0": "dbg_htw",
                                    "dbg_xq2r": "dbg_xq2w"}[wm])
                        if p.passno == 2:
                            gpsimd.wait_ge(p.sems["sTW"], v_)
                for (u, t, h) in steps_all[g * 8:(g + 1) * 8]:
                    if u < 2:
                        p.wait(gpsimd, "pt_mm_15")
                    if u >= 2:
                        p.wait(gpsimd, f"c3u_{u-2}")
                    i_ = gpsimd.dma_start(out=xpin[:, u % 2, :],
                                          in_=xprime_d[t * 128:(t + 1) * 128,
                                                       h * 512:(h + 1) * 512])
                    p.inc(i_, f"sXI{u % 2}", 16, f"cxp2_{u}")
                    if u >= 1:
                        pu, pt, ph = steps_all[u - 1]
                        p.wait(gpsimd, f"c3u_{u-1}")
                        i_ = gpsimd.dma_start(out=y_d[pt * 128:(pt + 1) * 128,
                                                      ph * 512:(ph + 1) * 512],
                                              in_=youts[:, (u - 1) % 2, :])
                        p.inc(i_, f"sYW{(u - 1) % 2}", 16, f"yw_{u-1}")
            p.wait(gpsimd, "c3u_31")
            i_ = gpsimd.dma_start(out=y_d[15 * 128:16 * 128, 512:1024],
                                  in_=youts[:, 31 % 2, :])
            p.inc(i_, "sYW1", 16, "yw_31")
            for s2 in ["sPW0", "sPW1", "sYW0", "sYW1"]:
                p.marks[f"fin_{s2}"] = (s2, p.counts[s2])

    return nc


def _emit_c3(V, vector, p, g, psS, osc4, youts, xpin, dbg):
    for h in range(2):
        for j in range(4):
            t = 4 * g + j
            u = g * 8 + h * 4 + j
            V.wait(f"c2mmj_{g}_{h}_{j}")
            V.wait(f"cxp2_{u}")
            if u >= 2:
                V.wait(f"yw_{u-2}")
            V.op(V.e.tensor_scalar(out=youts[:, u % 2, :],
                                   in0=psS[:, j * 512:(j + 1) * 512],
                                   scalar1=osc4[:, j:j + 1], scalar2=None,
                                   op0=ALU.mult))
            V.op(V.e.tensor_add(youts[:, u % 2, :], youts[:, u % 2, :],
                                xpin[:, u % 2, :]), f"c3_{t}_{h}")
            p.marks[f"c3u_{u}"] = p.marks[f"c3_{t}_{h}"]


def _emit_qkv_ev(V, vector, p, ev, qf, kf, vtok, bcols, bvbc, psS):
    kind, a = ev
    if kind in ("q", "k"):
        dst = qf if kind == "q" else kf
        jb = 0 if kind == "q" else 1
        V.wait(f"qk_mm_{kind}_{a}")
        if kind == "q" and a == 0:
            V.wait("a0_done")
        V.op(vector.tensor_scalar(out=dst[:, a, :], in0=psS[:, :],
                                  scalar1=bcols[:, jb, a:a + 1],
                                  scalar2=SCL, op0=ALU.add, op1=ALU.mult),
             f"qk_ev_{kind}_{a}")
    else:
        V.wait(f"v_mm_{a}")
        V.op(vector.scalar_tensor_tensor(out=vtok[:, a, :], in0=psS[:, 0:1024],
                                         scalar=1.0, in1=bvbc[:],
                                         op0=ALU.mult, op1=ALU.add), f"v_ev_{a}")


_PROGRAM_CACHE = {}


def _build_program():
    marks = {}
    build_nc(1, marks)
    return build_nc(2, marks)


def kernel(L0_attn_x, L0_attn_Wq, L0_attn_bq, L0_attn_Wk, L0_attn_bk,
           L0_attn_Wv, L0_attn_bv, L0_attn_Wo, L0_attn_bo,
           L0_ffn_w_gate, L0_ffn_w_up, L0_ffn_w_down):
    if "nc" not in _PROGRAM_CACHE:
        _PROGRAM_CACHE["nc"] = _build_program()
    nc = _PROGRAM_CACHE["nc"]
    x = np.ascontiguousarray(np.asarray(L0_attn_x, dtype=np.float32))

    def tr(a):
        return np.ascontiguousarray(np.asarray(a, np.float32).T)

    common = {
        "wq": tr(L0_attn_Wq), "wk": tr(L0_attn_Wk),
        "wv": tr(L0_attn_Wv), "wo": tr(L0_attn_Wo),
        "bq": np.ascontiguousarray(np.asarray(L0_attn_bq, np.float32)),
        "bk": np.ascontiguousarray(np.asarray(L0_attn_bk, np.float32)),
        "bv": np.ascontiguousarray(np.asarray(L0_attn_bv, np.float32)),
        "bo": np.ascontiguousarray(np.asarray(L0_attn_bo, np.float32)),
        "wg": tr(L0_ffn_w_gate), "wu": tr(L0_ffn_w_up), "wd": tr(L0_ffn_w_down),
        "identb": np.eye(128, dtype=ml_dtypes.bfloat16),
    }
    in_maps = []
    for c in range(NCORES):
        m = dict(common)
        m["x"] = np.ascontiguousarray(x[c])
        in_maps.append(m)
    res = run_bass_kernel_spmd(nc, in_maps, core_ids=list(range(NCORES)))
    y = np.stack([np.asarray(r["y"]) for r in res.results], axis=0)
    if DEBUG:
        global DEBUG_OUT
        DEBUG_OUT = res.results
    return y


# version 16
# speedup vs baseline: 1.0529x; 1.0529x over previous
import os
import numpy as np
import ml_dtypes
from contextlib import ExitStack

import concourse.bass as bass
import concourse.mybir as mybir
from concourse.bass_utils import run_bass_kernel_spmd
import concourse.bass_utils as _bu

if not getattr(_bu, "_ldw_patch", False):
    _orig_run_command = _bu.run_command

    def _run_command_ldw(cmd, **kw):
        if isinstance(cmd, list):
            cmd = ["--enable-ldw-opt=true" if c == "--enable-ldw-opt=false" else c
                   for c in cmd]
        return _orig_run_command(cmd, **kw)

    _bu.run_command = _run_command_ldw
    _bu._ldw_patch = True

F32 = mybir.dt.float32
BF16 = mybir.dt.bfloat16
U8 = mybir.dt.uint8
FP8 = mybir.dt.float8e4
AF = mybir.ActivationFunctionType
ALU = mybir.AluOpType
AX = mybir.AxisListType

S = 2048
E = 1024
INNER = 4096
NT = 16
ECH = 8
SCL = float(np.float32(1.0) / np.float32(np.sqrt(np.float32(32.0))))
MAGIC = 12582912.0
EPS = 1e-5
NEG = -1e30
NCORES = 8
KB = 1024

DEBUG = bool(int(os.environ.get("KDEBUG", "0")))
DEBUG_OUT = None

TERN_W = [("g", E, INNER), ("u", E, INNER), ("d", INNER, E)]


def tern_blocks(R, C):
    nh = max(1, C // 2048)
    cw = min(C, 2048)
    idx = 0
    for rb in range(R // 128):
        for h in range(nh):
            yield idx, rb, h, cw
            idx += 1


TERN_LIST = []
_bi = 0
for _nm, _R, _C in TERN_W:
    for _idx, _rb, _h, _cw in tern_blocks(_R, _C):
        TERN_LIST.append((_bi, _nm, _idx, _rb, _h, _cw))
        _bi += 1
NTERN = len(TERN_LIST)


class P:
    def __init__(self, passno, marks):
        self.passno = passno
        self.marks = marks
        self.counts = {}
        self.sems = {}

    def add_sem(self, nc, ctx, name):
        self.sems[name] = ctx.enter_context(nc.semaphore(name))
        self.counts[name] = 0

    def inc(self, instr, semname, n=1, mark=None):
        self.counts[semname] += n
        instr.then_inc(self.sems[semname], n)
        if mark is not None:
            self.marks[mark] = (semname, self.counts[semname])
        return self.counts[semname]

    def wait(self, eng, mark):
        if mark == "noop" or self.passno == 1:
            return
        semname, val = self.marks[mark]
        eng.wait_ge(self.sems[semname], val)


class Chain:
    def __init__(self, eng, p, sem, selfwait=True):
        self.e = eng
        self.p = p
        self.sem = sem
        self.selfwait = selfwait

    def wait(self, mark):
        self.p.wait(self.e, mark)

    def op(self, instr, mark=None):
        v = self.p.inc(instr, self.sem, 1, mark)
        if self.selfwait and self.p.passno == 2:
            self.e.wait_ge(self.p.sems[self.sem], v)
        return instr


def build_nc(passno, marks):
    nc = bass.Bass("TRN2", target_bir_lowering=False)
    p = P(passno, marks)

    x_d = nc.dram_tensor("x", [S, E], F32, kind="ExternalInput").ap()
    w_d = {}
    for nm in ["wq", "wk", "wv", "wo"]:
        w_d[nm] = nc.dram_tensor(nm, [E, E], F32, kind="ExternalInput").ap()
    b_d = {}
    for nm in ["bq", "bk", "bv", "bo"]:
        b_d[nm] = nc.dram_tensor(nm, [E], F32, kind="ExternalInput").ap()
    wf_d = {"g": nc.dram_tensor("wg", [E, INNER], F32, kind="ExternalInput").ap(),
            "u": nc.dram_tensor("wu", [E, INNER], F32, kind="ExternalInput").ap(),
            "d": nc.dram_tensor("wd", [INNER, E], F32, kind="ExternalInput").ap()}
    identb_d = nc.dram_tensor("identb", [128, 128], BF16, kind="ExternalInput").ap()
    y_d = nc.dram_tensor("y", [S, E], F32, kind="ExternalOutput").ap()

    skind = "ExternalOutput" if DEBUG else "Internal"
    tern_d = {"g": nc.dram_tensor("terng", [E, INNER], FP8, kind=skind).ap(),
              "u": nc.dram_tensor("ternu", [E, INNER], FP8, kind=skind).ap(),
              "d": nc.dram_tensor("ternd", [INNER, E], FP8, kind=skind).ap()}
    xprime_d = nc.dram_tensor("xprime", [S, E], F32, kind=skind).ap()
    if DEBUG:
        dbg_qf_d = nc.dram_tensor("dbg_qf", [128, ECH * S], BF16, kind="ExternalOutput").ap()
        dbg_kf_d = nc.dram_tensor("dbg_kf", [128, ECH * S], BF16, kind="ExternalOutput").ap()
        dbg_v_d = nc.dram_tensor("dbg_v", [128, NT * E], BF16, kind="ExternalOutput").ap()
        dbg_m8_d = nc.dram_tensor("dbg_m8", [128, NT * 32], F32, kind="ExternalOutput").ap()
        dbg_z_d = nc.dram_tensor("dbg_z", [128, NT * 8], F32, kind="ExternalOutput").ap()
        dbg_xq1_d = nc.dram_tensor("dbg_xq1", [128, E], BF16, kind="ExternalOutput").ap()
        dbg_htil_d = nc.dram_tensor("dbg_htil", [128, INNER], F32, kind="ExternalOutput").ap()
        dbg_xq2_d = nc.dram_tensor("dbg_xq2", [128, INNER], BF16, kind="ExternalOutput").ap()

    with ExitStack() as ctx:
        identb = ctx.enter_context(nc.sbuf_tensor("sb_ident", [128, 128], BF16))
        onesf = ctx.enter_context(nc.sbuf_tensor("onesf", [128, 1], F32))
        onesrow = ctx.enter_context(nc.sbuf_tensor("onesrow", [1, 128], F32))
        epscol = ctx.enter_context(nc.sbuf_tensor("epscol", [128, 1], F32))
        bcols = ctx.enter_context(nc.sbuf_tensor("bcols", [128, 2, ECH], F32))
        stats = ctx.enter_context(nc.sbuf_tensor("stats", [128, 64], F32))
        wsc = ctx.enter_context(nc.sbuf_tensor("wsc", [128, 16], F32))
        mv = ctx.enter_context(nc.sbuf_tensor("mv", [128, 16], F32))
        m8 = ctx.enter_context(nc.sbuf_tensor("m8", [128, 32], F32))
        zr = ctx.enter_context(nc.sbuf_tensor("zr", [128, 8], F32))
        osc4 = ctx.enter_context(nc.sbuf_tensor("osc4", [128, 4], F32))
        tA = ctx.enter_context(nc.sbuf_tensor("tA", [128, 4], F32))
        bnb = ctx.enter_context(nc.sbuf_tensor("bnb", [128, 12], F32))
        zc = ctx.enter_context(nc.sbuf_tensor("zc", [128, 2, 8], F32))
        if DEBUG:
            dbgm8 = ctx.enter_context(nc.sbuf_tensor("dbgm8", [128, NT, 32], F32))
            dbgz = ctx.enter_context(nc.sbuf_tensor("dbgz", [128, NT, 8], F32))

        AR = 203 * KB
        arena_t = ctx.enter_context(nc.sbuf_tensor("arena", [128, AR], U8))
        arena = arena_t[:]

        def carve(off_kb, kbytes, dt, inner_shape):
            off = int(off_kb * KB)
            ap = arena[:, off:off + int(kbytes * KB)].bitcast(dt)
            if inner_shape is None or len(inner_shape) == 1:
                return ap
            dims = " ".join(chr(ord('a') + i) for i in range(len(inner_shape)))
            kw = {chr(ord('a') + i): inner_shape[i] for i in range(len(inner_shape))}
            return ap.rearrange(f"p ({dims}) -> p {dims}", **kw)

        xn1T = carve(0, 32, BF16, [ECH, S])
        scA = carve(0, 8, F32, [S])
        scB = carve(8, 8, F32, [S])
        swork = carve(16, 8, F32, [S])
        e1b = carve(24, 4, BF16, [S])
        e2b = carve(28, 4, BF16, [S])
        qf = carve(32, 32, BF16, [ECH, S])
        kf = carve(64, 32, BF16, [ECH, S])
        vtok = carve(96, 32, BF16, [NT, E])
        wto = carve(128, 16, BF16, [ECH, E])
        wtt = carve(144, 16, BF16, [ECH, E])
        pbuf = carve(160, 4, BF16, [S])
        pbufB = carve(192, 4, BF16, [S])
        ptb = carve(164, 4, BF16, [NT, 128])
        aob = carve(168, 2, BF16, [E])
        aotb = carve(170, 2, BF16, [ECH, 128])
        xtile = carve(172, 8, F32, [2, E])
        xplus = carve(180, 4, F32, [E])
        xpst = carve(184, 8, F32, [2, E])
        bvbc = carve(192, 4, F32, [E])
        bobc = carve(196, 4, F32, [E])
        xnbf = carve(200, 2, BF16, [E])
        fwm = carve(96, 24, F32, [3, 2048])
        fwt = carve(176, 16, F32, [2, 2048])
        sgn2 = carve(128, 4, FP8, [2, 2048])
        msk2 = carve(132, 4, FP8, [2, 2048])
        terng = carve(0, 32, FP8, [ECH, INNER])
        ternu = carve(32, 32, FP8, [ECH, INNER])
        qstg = carve(64, 32, BF16, [8, 2048])
        ternd_r = carve(96, 32, FP8, [32, 1024])
        htil = carve(128, 16, F32, [INNER])
        xq2 = carve(144, 8, BF16, [INNER])
        cxa2 = carve(152, 8, F32, [2, E])
        xn2f = carve(160, 4, F32, [E])
        xq1 = carve(164, 2, BF16, [E])
        xq1T = carve(172, 4, BF16, [2, ECH, 128])
        silu2 = carve(168, 4, F32, [2, 512])
        xpin = carve(192, 4, F32, [2, 512])
        youts = carve(196, 4, F32, [2, 512])

        psS = ctx.enter_context(nc.psum_tensor("psS", [128, 2048], F32))
        psT = ctx.enter_context(nc.psum_tensor("psT", [128, 2048], BF16))
        psA = ctx.enter_context(nc.psum_tensor("psA", [128, 512], F32))
        psB = ctx.enter_context(nc.psum_tensor("psB", [128, 512], F32))

        for s_ in ["sPE", "sDV", "sSC", "sPO", "sC", "sX0", "sX1", "sF0", "sF1",
                   "sF2", "sG0", "sG1", "sW", "sTR", "sDS0", "sCX",
                   "sXI0", "sXI1", "sPW0", "sPW1", "sYW0", "sYW1", "sTW", "sTW0", "sTW1",
                   "sCX1"]:
            p.add_sem(nc, ctx, s_)
        block = ctx.enter_context(nc.Block())

        LASTT = TERN_LIST[-1]
        LAST_SGN = f"a2sgn_{LASTT[0]}"

        @block.sync
        def _(sync):
            i_ = sync.dma_start(out=identb[:], in_=identb_d)
            p.inc(i_, "sC", 16)
            i_ = sync.dma_start(out=bvbc[:], in_=b_d["bv"].unsqueeze(0).to_broadcast([128, E]))
            p.inc(i_, "sC", 16)
            i_ = sync.dma_start(out=bobc[:], in_=b_d["bo"].unsqueeze(0).to_broadcast([128, E]))
            p.inc(i_, "sC", 16)
            for j, nm in enumerate(["bq", "bk"]):
                for c in range(ECH):
                    i_ = sync.dma_start(out=bcols[:, j, c:c + 1],
                                        in_=b_d[nm][c * 128:(c + 1) * 128].unsqueeze(1))
                    p.inc(i_, "sC", 16)
            p.marks["a0_done"] = ("sC", p.counts["sC"])

            a1_iter = iter(range(NT))
            for k, (bi, nm, idx, rb, h, cw) in enumerate(TERN_LIST):
                if k % 4 == 0:
                    i = next(a1_iter, None)
                    if i is not None:
                        if i >= 2:
                            p.wait(sync, f"a1xnb_{i-2}")
                        i_ = sync.dma_start(out=xtile[:, i % 2, :],
                                            in_=x_d[i * 128:(i + 1) * 128, :])
                        p.inc(i_, f"sX{i % 2}", 16, f"a1x_{i}")
                if bi >= 3:
                    p.wait(sync, f"a2m_{bi-3}")
                i_ = sync.dma_start(out=fwm[:, bi % 3, 0:cw],
                                    in_=wf_d[nm][rb * 128:(rb + 1) * 128,
                                                 h * 2048:h * 2048 + cw])
                p.inc(i_, f"sF{bi % 3}", 16, f"a2mld_{bi}")

            for bi, nm, idx, rb, h, cw in TERN_LIST:
                if bi >= 2:
                    p.wait(sync, f"a2sgn_{bi-2}")
                i_ = sync.dma_start(out=fwt[:, bi % 2, 0:cw],
                                    in_=wf_d[nm][rb * 128:(rb + 1) * 128,
                                                 h * 2048:h * 2048 + cw])
                p.inc(i_, f"sG{bi % 2}", 16, f"a2tld_{bi}")
                if bi >= 1:
                    pbi, pnm, pidx, prb, ph, pcw = TERN_LIST[bi - 1]
                    p.wait(sync, f"a2tern_{pbi}")
                    i_ = sync.dma_start(out=tern_d[pnm][prb * 128:(prb + 1) * 128,
                                                        ph * 2048:ph * 2048 + pcw],
                                        in_=sgn2[:, pbi % 2, 0:pcw])
                    p.inc(i_, f"sTW{pbi % 2}", 16, f"a2out_{pbi}")
            p.wait(sync, f"a2tern_{LASTT[0]}")
            i_ = sync.dma_start(out=tern_d[LASTT[1]][LASTT[3] * 128:(LASTT[3] + 1) * 128,
                                                     LASTT[4] * 2048:LASTT[4] * 2048 + LASTT[5]],
                                in_=sgn2[:, LASTT[0] % 2, 0:LASTT[5]])
            p.inc(i_, f"sTW{LASTT[0] % 2}", 16, f"a2out_{LASTT[0]}")
            p.marks["tern_all0"] = ("sTW0", p.counts["sTW0"])
            p.marks["tern_all1"] = ("sTW1", p.counts["sTW1"])

            for i in range(NT):
                if i < 2:
                    p.wait(sync, f"a1xnb_{14+i}")
                    p.wait(sync, "tern_all0")
                    p.wait(sync, "tern_all1")
                else:
                    p.wait(sync, f"xplus_{i-2}")
                i_ = sync.dma_start(out=xtile[:, i % 2, :], in_=x_d[i * 128:(i + 1) * 128, :])
                p.inc(i_, f"sX{i % 2}", 16, f"bx_{i}")
                if i >= 1:
                    p.wait(sync, f"xp_{i-1}_1")
                    i_ = sync.dma_start(out=xprime_d[(i - 1) * 128:i * 128, :],
                                        in_=xpst[:, (i - 1) % 2, :])
                    p.inc(i_, f"sPW{(i - 1) % 2}", 16, f"xpw_{i-1}")
            p.wait(sync, "xp_15_1")
            i_ = sync.dma_start(out=xprime_d[15 * 128:16 * 128, :], in_=xpst[:, 1, :])
            p.inc(i_, "sPW1", 16, "xpw_15")

            p.wait(sync, "exp_15")
            p.wait(sync, "P_15")
            p.wait(sync, "sc_mm_15")
            i_ = sync.dma_start(out=terng[:, :, :],
                               in_=tern_d["g"].rearrange("(c p) i -> p c i", p=128))
            p.inc(i_, "sTR", 16)
            i_ = sync.dma_start(out=ternu[:, :, :],
                               in_=tern_d["u"].rearrange("(c p) i -> p c i", p=128))
            p.inc(i_, "sTR", 16, "ld_ternu")
            p.wait(sync, "av_mm_15_1")
            i_ = sync.dma_start(out=ternd_r[:, :, :],
                                in_=tern_d["d"].rearrange("(ic p) e -> p ic e", p=128))
            p.inc(i_, "sDS0", 16, "ld_ternd")

            for g in range(4):
                for j in range(4):
                    t = 4 * g + j
                    p.wait(sync, f"xpw_{t}")
                    if t == 0:
                        p.wait(sync, "xplus_15")
                    if t >= 2:
                        p.wait(sync, f"c1ln_{t-2}")
                    i_ = sync.dma_start(out=cxa2[:, t % 2, :],
                                        in_=xprime_d[t * 128:(t + 1) * 128, :])
                    p.inc(i_, "sCX" if t % 2 == 0 else "sCX1", 16, f"cxp_{t}")
            if DEBUG:
                p.wait(sync, "dbg_ready")
                for ap_, dd in [(qf, dbg_qf_d), (kf, dbg_kf_d), (vtok, dbg_v_d)]:
                    i_ = sync.dma_start(out=dd, in_=ap_[:, :, :])
                    p.inc(i_, "sC", 16)
                i_ = sync.dma_start(out=dbg_m8_d, in_=dbgm8[:, :, :])
                p.inc(i_, "sC", 16)
                i_ = sync.dma_start(out=dbg_z_d, in_=dbgz[:, :, :])
                p.inc(i_, "sC", 16)
                p.marks["dbg_st"] = ("sC", p.counts["sC"])
                p.wait(sync, "dbg_st")
            for fm in ["fin_sPW0", "fin_sPW1", "fin_sYW0", "fin_sYW1"]:
                p.wait(sync, fm)

        @block.tensor
        def _(tensor):
            T = Chain(tensor, p, "sPE", selfwait=False)
            T.wait("a0_done")
            for i in range(NT):
                T.wait(f"a1xnb_{i}")
                if i >= 1:
                    T.wait(f"a1xnT_ev_{i-1}")
                for c in range(ECH):
                    i_ = tensor.transpose(psT[:, c * 128:(c + 1) * 128],
                                          xnbf[:, c * 128:(c + 1) * 128], identb[:])
                T.op(i_, f"a1xnT_{i}")
            T.wait("a2r_done")
            for wi in range(3):
                i_ = tensor.matmul(psA[0:1, wi:wi + 1], tA[:, wi:wi + 1], onesf[:],
                                   start=True, stop=True)
            T.op(i_, "a2tot_mm")
            T.wait("a2tot_ev")
            i_ = tensor.matmul(psB[:, 0:4], onesrow[:], tA[0:1, 0:4], start=True, stop=True)
            T.op(i_, "a2bc_mm")
            for nm in ["q", "k"]:
                T.wait(f"wld_{nm}")
                T.wait("a1xnT_ev_15")
                dst = qf if nm == "q" else kf
                for oc in range(ECH):
                    if oc >= 1:
                        T.wait(f"qk_ev_{nm}_{oc-1}")
                    elif nm == "k":
                        T.wait("qk_ev_q_7")
                    for sc in range(4):
                        for c in range(ECH):
                            i_ = tensor.matmul(psS[:, sc * 512:(sc + 1) * 512],
                                               wtt[:, c, oc * 128:(oc + 1) * 128],
                                               xn1T[:, c, sc * 512:(sc + 1) * 512],
                                               start=(c == 0), stop=(c == ECH - 1))
                    T.op(i_, f"qk_mm_{nm}_{oc}")
            T.wait("wld_v")
            T.wait("qk_ev_k_7")
            for st in range(NT):
                if st >= 1:
                    T.wait(f"v_ev_{st-1}")
                for oh in range(2):
                    for c in range(ECH):
                        i_ = tensor.matmul(psS[:, oh * 512:(oh + 1) * 512],
                                           xn1T[:, c, st * 128:(st + 1) * 128],
                                           wtt[:, c, oh * 512:(oh + 1) * 512],
                                           start=(c == 0), stop=(c == ECH - 1))
                T.op(i_, f"v_mm_{st}")
            def pe_scores(i):
                for sc in range(4):
                    for c in range(ECH):
                        i_ = tensor.matmul(psS[:, sc * 512:(sc + 1) * 512],
                                           qf[:, c, i * 128:(i + 1) * 128],
                                           kf[:, c, sc * 512:(sc + 1) * 512],
                                           start=(c == 0), stop=(c == ECH - 1))
                    T.op(i_, f"sc_mm_{i}_{sc}")
                p.marks[f"sc_mm_{i}"] = ("sPE", p.counts["sPE"])

            T.wait("v_ev_15")
            pe_scores(0)
            T.wait("sc_ev_0")
            pe_scores(1)
            for t in range(NT):
                T.wait(f"P_{t}")
                T.wait(f"pt_ev_{t-1}" if t >= 1 else "a1xnT_ev_15")
                pb = pbuf if t % 2 == 0 else pbufB
                for c in range(NT):
                    i_ = tensor.transpose(psT[:, c * 128:(c + 1) * 128],
                                          pb[:, c * 128:(c + 1) * 128], identb[:])
                T.op(i_, f"pt_mm_{t}")
                T.wait(f"pt_ev_{t}")
                for oh in range(2):
                    if oh == 0:
                        T.wait(f"xp_{t-1}_0" if t >= 1 else "a2tot_ev")
                    else:
                        T.wait(f"xp_{t-1}_1" if t >= 1 else "a2sc_done")
                    avdst = psA if oh == 0 else psB
                    for kt in range(NT):
                        i_ = tensor.matmul(avdst[:, :], ptb[:, kt, :],
                                           vtok[:, kt, oh * 512:(oh + 1) * 512],
                                           start=(kt == 0), stop=(kt == NT - 1))
                    T.op(i_, f"av_mm_{t}_{oh}")
                if t + 2 < NT:
                    T.wait(f"sc_ev_{t+1}")
                    pe_scores(t + 2)
                T.wait(f"av_ev_{t}_1")
                for c in range(ECH):
                    i_ = tensor.transpose(psT[:, c * 128:(c + 1) * 128],
                                          aob[:, c * 128:(c + 1) * 128], identb[:])
                T.op(i_, f"aot_mm_{t}")
                T.wait(f"aot_ev_{t}")
                if t == 0:
                    T.wait("wld_o")
                for oh in range(2):
                    if oh == 0:
                        T.wait(f"av_ev_{t}_0")
                    odst = psA if oh == 0 else psB
                    for c in range(ECH):
                        i_ = tensor.matmul(odst[:, :], aotb[:, c, :],
                                           wto[:, c, oh * 512:(oh + 1) * 512],
                                           start=(c == 0), stop=(c == ECH - 1))
                    T.op(i_, f"o_mm_{t}_{oh}")
            def pe_xq1T(t):
                T.wait(f"xq1_{t}")
                T.wait(f"xq2T_ev_{t-2}_1" if t >= 2 else
                       ("xq1T_ev_0" if t == 1 else "aot_ev_15"))
                for c in range(ECH):
                    i_ = tensor.transpose(psT[:, c * 128:(c + 1) * 128],
                                          xq1[:, c * 128:(c + 1) * 128], identb[:])
                T.op(i_, f"xq1T_mm_{t}")

            def pe_xq2T(t):
                T.wait(f"xq2_{t}")
                T.wait(f"xq1T_ev_{t+1}" if t + 1 < NT else "xq2T_ev_14_1")
                for half in range(2):
                    if half == 1:
                        T.wait(f"xq2T_ev_{t}_0")
                    for c in range(16):
                        i_ = tensor.transpose(
                            psT[:, c * 128:(c + 1) * 128],
                            xq2[:, half * 2048 + c * 128:half * 2048 + (c + 1) * 128],
                            identb[:])
                    T.op(i_, f"xq2T_mm_{t}_{half}")

            pe_xq1T(0)
            for g in range(4):
                for j in range(4):
                    t = 4 * g + j
                    T.wait(f"xq1T_ev_{t}")
                    if t == 0:
                        T.wait("ld_ternu")
                        T.wait("sc_ev_15")
                    for ib in range(8):
                        gbi = t * 8 + ib
                        T.wait(f"gu_ev_{gbi-3}" if gbi >= 3 else "noop")
                        if g >= 1 and j == 0 and ib == 0:
                            for uu in range(4):
                                T.wait(f"c3u_{8*(g-1)+4+uu}")
                        r3 = gbi % 3
                        gps = (psA[:, :] if r3 == 0 else
                               psS[:, 0:512] if r3 == 1 else psS[:, 1024:1536])
                        ups = (psB[:, :] if r3 == 0 else
                               psS[:, 512:1024] if r3 == 1 else psS[:, 1536:2048])
                        for c in range(ECH):
                            i_ = tensor.matmul(gps, xq1T[:, t % 2, c, :],
                                               terng[:, c, ib * 512:(ib + 1) * 512],
                                               start=(c == 0), stop=(c == ECH - 1))
                        T.op(i_, f"g_mm_{gbi}")
                        for c in range(ECH):
                            i_ = tensor.matmul(ups, xq1T[:, t % 2, c, :],
                                               ternu[:, c, ib * 512:(ib + 1) * 512],
                                               start=(c == 0), stop=(c == ECH - 1))
                        T.op(i_, f"u_mm_{gbi}")
                        if ib == 2 and t >= 1 and (t - 1) % 4 != 3:
                            pe_xq2T(t - 1)
                        if ib == 6 and t + 1 < NT:
                            pe_xq1T(t + 1)
                    if j == 3:
                        base = 32 * g
                        lr1 = max(b for b in range(base, base + 32) if b % 3 == 1)
                        lr2 = max(b for b in range(base, base + 32) if b % 3 == 2)
                        for h in range(2):
                            for j2 in range(4):
                                if j2 == 0 and h == 0 and g == 0:
                                    T.wait("ld_ternd")
                                if h == 1:
                                    T.wait(f"c3_{4*g+j2}_0")
                                elif g >= 1:
                                    T.wait(f"c3_{4*(g-1)+j2}_1")
                                if h == 0:
                                    T.wait(f"gu_ev_{lr1 if j2 < 2 else lr2}")
                                    T.wait(f"xq2T_ev_{4*g+j2}_0")
                                    T.wait(f"xq2T_ev_{4*g+j2}_1")
                                for ic in range(32):
                                    i_ = tensor.matmul(
                                        psS[:, j2 * 512:(j2 + 1) * 512],
                                        qstg[:, j2 * 2 + ic // 16,
                                             (ic % 16) * 128:(ic % 16 + 1) * 128],
                                        ternd_r[:, ic, h * 512:(h + 1) * 512],
                                        start=(ic == 0), stop=(ic == 31))
                                T.op(i_, f"c2mmj_{g}_{h}_{j2}")
                                if h == 0 and j2 == 1:
                                    pe_xq2T(t)
                            p.marks[f"c2mm_{g}_{h}"] = ("sPE", p.counts["sPE"])

        @block.vector
        def _(vector):
            V = Chain(vector, p, "sDV")
            V.op(vector.memset(onesf[:], 1.0))
            V.op(vector.memset(onesrow[:], 1.0))
            V.op(vector.memset(epscol[:], EPS), "consts")
            if DEBUG:
                V.op(vector.memset(dbgm8[:, :, :], 0.0))
                V.op(vector.memset(dbgz[:, :, :], 0.0))

            def ln_stats(src):
                V.op(vector.bn_stats(bnb[:, 0:6], src[:, 0:512]))
                V.op(vector.bn_stats(bnb[:, 6:12], src[:, 512:1024]))
                V.op(vector.bn_aggr(mv[:, 12:14], bnb[:, 0:12]))

            base = {"g": 0, "u": 16, "d": 32}
            a1_iter = iter(range(NT))
            for k, (bi, nm, idx, rb, h, cw) in enumerate(TERN_LIST):
                if k % 4 == 0:
                    i = next(a1_iter, None)
                    if i is not None:
                        V.wait(f"a1x_{i}")
                        ln_stats(xtile[:, i % 2, :])
                        p.marks[f"a1bn_{i}"] = ("sDV", p.counts["sDV"])
                        V.wait(f"a1rstd_{i}")
                        V.op(vector.tensor_scalar(out=xnbf[:], in0=xtile[:, i % 2, :],
                                                  scalar1=mv[:, 12:13], scalar2=mv[:, 13:14],
                                                  op0=ALU.subtract, op1=ALU.mult),
                             f"a1xnb_{i}")
                        V.wait(f"a1xnT_{i}")
                        V.op(vector.tensor_copy(xn1T[:, :, i * 128:(i + 1) * 128],
                                                psT[:, 0:1024].rearrange("p (c t) -> p c t", c=8)),
                             f"a1xnT_ev_{i}")
                V.wait(f"a2mld_{bi}")
                V.op(vector.reduce_sum(stats[:, base[nm] + idx:base[nm] + idx + 1],
                                       fwm[:, bi % 3, 0:cw], axis=AX.X,
                                       apply_absolute_value=True), f"a2m_{bi}")
            for wi, (nm, R, C) in enumerate(TERN_W):
                nidx = (R // 128) * max(1, C // 2048)
                V.op(vector.reduce_sum(tA[:, wi:wi + 1], stats[:, base[nm]:base[nm] + nidx],
                                       axis=AX.X))
            V.op(vector.tensor_copy(tA[:, 3:4], tA[:, 0:1]), "a2r_done")
            V.wait("a2tot_mm")
            V.op(vector.tensor_copy(tA[0:1, 0:3], psA[0:1, 0:3]), "a2tot_ev")
            V.wait("a2bc_mm")
            NW = float(INNER * E)
            V.op(vector.tensor_scalar(out=wsc[:, 0:3], in0=psB[:, 0:3],
                                      scalar1=1.0 / NW, scalar2=EPS,
                                      op0=ALU.mult, op1=ALU.max))
            V.op(vector.tensor_scalar_mul(wsc[:, 3:6], wsc[:, 0:3], 0.5))
            V.op(vector.tensor_scalar_mul(wsc[:, 9:12], wsc[:, 3:6], -1.0))
            V.op(vector.tensor_scalar_mul(wsc[:, 12:13], wsc[:, 0:1], 0.5))
            V.op(vector.reciprocal(wsc[:, 6:9], wsc[:, 0:3]), "a2sc_done")

            qkv_ev = ([("q", oc) for oc in range(ECH)] + [("k", oc) for oc in range(ECH)]
                      + [("v", st) for st in range(NT)])
            ev_iter = iter(qkv_ev)
            for bi, nm, idx, rb, h, cw in TERN_LIST:
                V.wait(f"a2sgn_{bi}")
                V.op(vector.tensor_add(sgn2[:, bi % 2, 0:cw], sgn2[:, bi % 2, 0:cw],
                                       msk2[:, bi % 2, 0:cw]), f"a2tern_{bi}")
                if bi % 2 == 1:
                    ev = next(ev_iter, None)
                    if ev is not None:
                        _emit_qkv_ev(V, vector, p, ev, qf, kf, vtok, bcols, bvbc, psS)
            for ev in ev_iter:
                _emit_qkv_ev(V, vector, p, ev, qf, kf, vtok, bcols, bvbc, psS)

            def emit_topk(t):
                sb = scA if t % 2 == 0 else scB
                V.wait(f"sc_ev_{t}")
                if t >= 1:
                    V.wait(f"exp_{t-1}")
                src = sb
                for r in range(4):
                    V.op(vector.max(out=m8[:, 8 * r:8 * r + 8], in_=src[:, :]))
                    if r == 0:
                        V.op(vector.tensor_scalar_mul(mv[:, 14:15], m8[:, 0:1], -1.0),
                             f"r1_{t}")
                    V.op(vector.match_replace(out=swork[:, :],
                                              in_to_replace=m8[:, 8 * r:8 * r + 8],
                                              in_values=src[:, :], imm_value=NEG))
                    src = swork
                    if t >= 1 and r == 1:
                        V.wait(f"av_mm_{t-1}_0")
                        V.op(vector.tensor_scalar_mul(aob[:, 0:512], psA[:, :],
                                                      zr[:, 1 + 2 * ((t - 1) % 2):2 + 2 * ((t - 1) % 2)]),
                             f"av_ev_{t-1}_0")
                    if t >= 1 and r == 2:
                        V.wait(f"av_mm_{t-1}_1")
                        V.op(vector.tensor_scalar_mul(aob[:, 512:1024], psB[:, :],
                                                      zr[:, 1 + 2 * ((t - 1) % 2):2 + 2 * ((t - 1) % 2)]),
                             f"av_ev_{t-1}_1")
                V.op(vector.tensor_copy(tA[:, 3:4], m8[:, 31:32]), f"tk_{t}")
                if DEBUG:
                    V.op(vector.tensor_copy(dbgm8[:, t, :], m8[:, :]))

            def emit_P(t):
                pb = pbuf if t % 2 == 0 else pbufB
                zc = 2 * (t % 2)
                V.wait(f"exp_{t}")
                V.op(vector.scalar_tensor_tensor(out=pb[:, :], in0=e1b[:, :], scalar=1.0,
                                                 in1=e2b[:, :], op0=ALU.mult,
                                                 op1=ALU.subtract, accum_out=zr[:, zc:zc + 1]),
                     f"P_{t}")
                V.op(vector.reciprocal(zr[:, zc + 1:zc + 2], zr[:, zc:zc + 1]))
                if DEBUG:
                    V.op(vector.tensor_copy(dbgz[:, t, 0:2], zr[:, zc:zc + 2]))

            def emit_xp(t):
                for oh in range(2):
                    V.wait(f"o_mm_{t}_{oh}")
                    V.wait(f"xplus_{t}")
                    if t == 0 and oh == 0:
                        V.wait("tern_all0")
                        V.wait("tern_all1")
                    if t >= 2 and oh == 0:
                        V.wait(f"xpw_{t-2}")
                    V.op(vector.scalar_tensor_tensor(
                        out=xpst[:, t % 2, oh * 512:(oh + 1) * 512],
                        in0=(psA if oh == 0 else psB)[:, :], scalar=1.0,
                        in1=xplus[:, oh * 512:(oh + 1) * 512],
                        op0=ALU.mult, op1=ALU.add), f"xp_{t}_{oh}")

            emit_topk(0)
            emit_P(0)
            for t in range(NT - 1):
                emit_topk(t + 1)
                emit_P(t + 1)
                emit_xp(t)
            t = NT - 1
            V.wait(f"av_mm_{t}_0")
            V.op(vector.tensor_scalar_mul(aob[:, 0:512], psA[:, :],
                                          zr[:, 1 + 2 * (t % 2):2 + 2 * (t % 2)]),
                 f"av_ev_{t}_0")
            V.wait(f"av_mm_{t}_1")
            V.op(vector.tensor_scalar_mul(aob[:, 512:1024], psB[:, :],
                                          zr[:, 1 + 2 * (t % 2):2 + 2 * (t % 2)]),
                 f"av_ev_{t}_1")
            emit_xp(NT - 1)

            def c_head(t):
                sl = t % 2
                V.wait(f"cxp_{t}")
                if t == 0:
                    V.wait("xpw_14")
                    V.wait("xpw_15")
                ln_stats(cxa2[:, sl, :])
                p.marks[f"c1bn_{t}"] = ("sDV", p.counts["sDV"])
                V.wait(f"c1rstd_{t}")
                V.op(vector.tensor_scalar(out=xn2f[:], in0=cxa2[:, sl, :],
                                          scalar1=mv[:, 12:13], scalar2=mv[:, 13:14],
                                          op0=ALU.subtract, op1=ALU.mult), f"c1ln_{t}")
                V.op(vector.reduce_max(zc[:, sl, 0:1], xn2f[:], axis=AX.X,
                                       apply_absolute_value=True))
                V.op(vector.tensor_scalar_max(zc[:, sl, 0:1], zc[:, sl, 0:1], EPS))
                V.op(vector.reciprocal(zc[:, sl, 1:2], zc[:, sl, 0:1]))
                V.op(vector.tensor_scalar_mul(zc[:, sl, 0:1], zc[:, sl, 1:2], 127.0))
                V.op(vector.reciprocal(zc[:, sl, 1:2], zc[:, sl, 0:1]))
                V.op(vector.tensor_mul(zc[:, sl, 2:3], zc[:, sl, 1:2], wsc[:, 12:13]),
                     f"cg_{t}")
                V.op(vector.tensor_scalar(out=xn2f[:], in0=xn2f[:],
                                          scalar1=zc[:, sl, 0:1], scalar2=MAGIC,
                                          op0=ALU.mult, op1=ALU.add))
                if t >= 1:
                    V.wait(f"xq1T_ev_{t-1}")
                if DEBUG and t == 1:
                    V.wait("dbg_xq1w")
                V.op(vector.tensor_scalar(out=xq1[:], in0=xn2f[:], scalar1=MAGIC,
                                          scalar2=None, op0=ALU.subtract), f"xq1_{t}")
                if DEBUG and t == 0:
                    p.marks["dbg_xq1r"] = ("sDV", p.counts["sDV"])

            def c_body(t):
                for ib in range(8):
                    gbi = t * 8 + ib
                    r3 = gbi % 3
                    ups = (psB[:, :] if r3 == 0 else
                           psS[:, 512:1024] if r3 == 1 else psS[:, 1536:2048])
                    V.wait(f"silu_{gbi}")
                    V.wait(f"u_mm_{gbi}")
                    V.op(vector.scalar_tensor_tensor(
                        out=htil[:, ib * 512:(ib + 1) * 512], in0=ups, scalar=1.0,
                        in1=silu2[:, gbi % 2, :], op0=ALU.mult, op1=ALU.mult),
                        f"gu_ev_{gbi}")
                    V.op(vector.reduce_max(stats[:, ib:ib + 1],
                                           htil[:, ib * 512:(ib + 1) * 512],
                                           axis=AX.X, apply_absolute_value=True))
                p.marks[f"htil_{t}"] = ("sDV", p.counts["sDV"])

            def c_tail(t, j):
                sl = t % 2
                V.op(vector.reduce_max(zc[:, sl, 3:4], stats[:, 0:8], axis=AX.X),
                     f"hmax_{t}")
                if DEBUG and t == 0:
                    V.wait("dbg_htw")
                V.op(vector.tensor_mul(tA[:, 0:1], zc[:, sl, 0:1], wsc[:, 7:8]))
                V.op(vector.tensor_scalar_mul(tA[:, 0:1], tA[:, 0:1], EPS * 2.0))
                V.op(vector.tensor_max(zc[:, sl, 3:4], zc[:, sl, 3:4], tA[:, 0:1]))
                V.op(vector.reciprocal(zc[:, sl, 4:5], zc[:, sl, 3:4]))
                V.op(vector.tensor_scalar_mul(zc[:, sl, 4:5], zc[:, sl, 4:5], 127.0))
                V.op(vector.tensor_mul(tA[:, 1:2], zc[:, sl, 3:4], zc[:, sl, 1:2]))
                V.op(vector.tensor_mul(tA[:, 1:2], tA[:, 1:2], wsc[:, 1:2]))
                V.op(vector.tensor_mul(tA[:, 1:2], tA[:, 1:2], wsc[:, 2:3]))
                V.op(vector.tensor_scalar_mul(osc4[:, j:j + 1], tA[:, 1:2],
                                              1.0 / (127.0 * 4.0)), f"osc_{t}")
                if DEBUG:
                    V.op(vector.tensor_copy(dbgz[:, t, 2:7], zc[:, sl, 0:5]))
                    V.op(vector.tensor_copy(dbgz[:, t, 7:8], osc4[:, j:j + 1]))
                V.op(vector.tensor_scalar(out=htil[:, :], in0=htil[:, :],
                                          scalar1=zc[:, sl, 4:5], scalar2=MAGIC,
                                          op0=ALU.mult, op1=ALU.add))
                if t >= 1:
                    V.wait(f"xq2T_mm_{t-1}_1")
                    if DEBUG and t == 1:
                        V.wait("dbg_xq2w")
                V.op(vector.tensor_scalar(out=xq2[:], in0=htil[:, :], scalar1=MAGIC,
                                          scalar2=None, op0=ALU.subtract), f"xq2_{t}")
                if DEBUG and t == 0:
                    p.marks["dbg_xq2r"] = ("sDV", p.counts["sDV"])

            c_head(0)
            for t in range(NT):
                g, j = t // 4, t % 4
                if t + 1 < NT:
                    c_head(t + 1)
                c_body(t)
                c_tail(t, j)
                if j == 3:
                    _emit_c3(V, vector, p, g, psS, osc4, youts, xpin, DEBUG)
            if DEBUG:
                V.op(vector.tensor_copy(tA[:, 0:1], tA[:, 0:1]), "dbg_ready")

        @block.scalar
        def _(scalar):
            C_ = Chain(scalar, p, "sSC")
            for i in range(NT):
                C_.wait(f"a1bn_{i}")
                C_.op(scalar.activation(out=mv[:, 15:16], in_=mv[:, 13:14], func=AF.Ln,
                                        bias=epscol[:], scale=1.0))
                C_.op(scalar.activation(out=mv[:, 13:14], in_=mv[:, 15:16], func=AF.Exp,
                                        scale=-0.5), f"a1rstd_{i}")
            for bi, nm, idx, rb, h, cw in TERN_LIST:
                wi = {"g": 0, "u": 1, "d": 2}[nm]
                C_.wait(f"a2tld_{bi}")
                if bi == 0:
                    C_.wait("a2sc_done")
                if bi >= 2:
                    C_.wait(f"a2out_{bi-2}")
                C_.op(scalar.activation(out=sgn2[:, bi % 2, 0:cw], in_=fwt[:, bi % 2, 0:cw],
                                        func=AF.Sign, bias=wsc[:, 9 + wi:10 + wi]))
                C_.op(scalar.activation(out=msk2[:, bi % 2, 0:cw], in_=fwt[:, bi % 2, 0:cw],
                                        func=AF.Sign, bias=wsc[:, 3 + wi:4 + wi]),
                      f"a2sgn_{bi}")
            def s_scev(i):
                nb = scA if i % 2 == 0 else scB
                for sc4 in range(4):
                    C_.wait(f"sc_mm_{i}_{sc4}")
                    C_.op(scalar.copy(nb[:, sc4 * 512:(sc4 + 1) * 512],
                                      psS[:, sc4 * 512:(sc4 + 1) * 512]),
                          f"sc_ev_{i}" if sc4 == 3 else None)

            C_.wait("sc_mm_0")
            C_.op(scalar.copy(scA[:, :], psS[:, :]), "sc_ev_0")
            C_.wait("r1_0")
            C_.op(scalar.activation(out=e1b[:, :], in_=scA[:, :], func=AF.Exp,
                                    bias=mv[:, 14:15], scale=1.0))
            s_scev(1)
            C_.wait("tk_0")
            C_.op(scalar.activation(out=e2b[:, :], in_=swork[:, :], func=AF.Exp,
                                    bias=mv[:, 14:15], scale=1.0), "exp_0")
            for t in range(NT):
                C_.wait(f"pt_mm_{t}")
                C_.op(scalar.copy(ptb[:, :, :],
                                  psT[:, :].rearrange("p (c t) -> p c t", t=128)),
                      f"pt_ev_{t}")
                if t + 1 < NT:
                    nb = scB if t % 2 == 0 else scA
                    C_.wait(f"r1_{t+1}")
                    C_.op(scalar.activation(out=e1b[:, :], in_=nb[:, :], func=AF.Exp,
                                            bias=mv[:, 14:15], scale=1.0))
                if t + 2 < NT:
                    s_scev(t + 2)
                C_.wait(f"aot_mm_{t}")
                C_.op(scalar.copy(aotb[:, :, :],
                                  psT[:, 0:1024].rearrange("p (c t) -> p c t", c=8)),
                      f"aot_ev_{t}")
                if t + 1 < NT:
                    C_.wait(f"tk_{t+1}")
                    C_.op(scalar.activation(out=e2b[:, :], in_=swork[:, :], func=AF.Exp,
                                            bias=mv[:, 14:15], scale=1.0), f"exp_{t+1}")
            def s_head(t):
                C_.wait(f"c1bn_{t}")
                C_.op(scalar.activation(out=mv[:, 15:16], in_=mv[:, 13:14], func=AF.Ln,
                                        bias=epscol[:], scale=1.0))
                C_.op(scalar.activation(out=mv[:, 13:14], in_=mv[:, 15:16], func=AF.Exp,
                                        scale=-0.5), f"c1rstd_{t}")

            def s_xq1T(t):
                C_.wait(f"xq1T_mm_{t}")
                C_.op(scalar.copy(xq1T[:, t % 2, :, :],
                                  psT[:, 0:1024].rearrange("p (c t) -> p c t", c=8)),
                      f"xq1T_ev_{t}")

            def s_xq2T(t):
                for half in range(2):
                    C_.wait(f"xq2T_mm_{t}_{half}")
                    if t >= 4:
                        C_.wait(f"c2mm_{t//4-1}_1")
                    C_.op(scalar.copy(qstg[:, (t % 4) * 2 + half, :], psT[:, :]),
                          f"xq2T_ev_{t}_{half}")

            s_head(0)
            s_xq1T(0)
            for g in range(4):
                for j in range(4):
                    t = 4 * g + j
                    if t + 1 < NT:
                        s_head(t + 1)
                    if t >= 1 and (t - 1) % 4 != 3:
                        s_xq2T(t - 1)
                    C_.wait(f"cg_{t}")
                    for ib in range(8):
                        gbi = t * 8 + ib
                        r3 = gbi % 3
                        gps = (psA[:, :] if r3 == 0 else
                               psS[:, 0:512] if r3 == 1 else psS[:, 1024:1536])
                        C_.wait(f"g_mm_{gbi}")
                        if gbi >= 2:
                            C_.wait(f"gu_ev_{gbi-2}")
                        C_.op(scalar.activation(out=silu2[:, gbi % 2, :], in_=gps,
                                                func=AF.Silu, scale=zc[:, t % 2, 2:3]),
                              f"silu_{gbi}")
                    if t + 1 < NT:
                        s_xq1T(t + 1)
                    if j == 3:
                        s_xq2T(t)

        @block.gpsimd
        def _(gpsimd):
            G = Chain(gpsimd, p, "sPO")
            for nm, gate in [("q", None), ("k", "qk_mm_q_7"), ("v", "qk_mm_k_7"),
                             ("o", "tern_all0")]:
                if gate:
                    p.wait(gpsimd, gate)
                    if gate == "tern_all0":
                        p.wait(gpsimd, "tern_all1")
                dst = wto if nm == "o" else wtt
                for ec in range(ECH):
                    i_ = gpsimd.dma_start(out=dst[:, ec, :],
                                          in_=w_d["w" + nm][ec * 128:(ec + 1) * 128, :])
                    p.inc(i_, "sW", 16)
                p.marks[f"wld_{nm}"] = ("sW", p.counts["sW"])
            G.wait("ld_bobc") if False else p.wait(gpsimd, "a0_done")
            for i in range(NT):
                G.wait(f"bx_{i}")
                if i >= 1:
                    G.wait(f"xp_{i-1}_1")
                G.op(gpsimd.tensor_add(xplus[:], xtile[:, i % 2, :], bobc[:]), f"xplus_{i}")
            steps_all = [(g_ * 8 + h_ * 4 + j_, 4 * g_ + j_, h_)
                         for g_ in range(4) for h_ in range(2) for j_ in range(4)]
            for g in range(4):
                if DEBUG and g == 0:
                    for wm, dd, ap_ in [("dbg_xq1r", dbg_xq1_d, xq1[:]),
                                        ("hmax_0", dbg_htil_d, htil[:, :]),
                                        ("dbg_xq2r", dbg_xq2_d, xq2[:])]:
                        p.wait(gpsimd, wm)
                        i_ = gpsimd.dma_start(out=dd, in_=ap_)
                        v_ = p.inc(i_, "sTW", 16,
                                   {"dbg_xq1r": "dbg_xq1w", "hmax_0": "dbg_htw",
                                    "dbg_xq2r": "dbg_xq2w"}[wm])
                        if p.passno == 2:
                            gpsimd.wait_ge(p.sems["sTW"], v_)
                for (u, t, h) in steps_all[g * 8:(g + 1) * 8]:
                    if u < 2:
                        p.wait(gpsimd, "pt_mm_15")
                    if u >= 2:
                        p.wait(gpsimd, f"c3u_{u-2}")
                    i_ = gpsimd.dma_start(out=xpin[:, u % 2, :],
                                          in_=xprime_d[t * 128:(t + 1) * 128,
                                                       h * 512:(h + 1) * 512])
                    p.inc(i_, f"sXI{u % 2}", 16, f"cxp2_{u}")
                    if u >= 1:
                        pu, pt, ph = steps_all[u - 1]
                        p.wait(gpsimd, f"c3u_{u-1}")
                        i_ = gpsimd.dma_start(out=y_d[pt * 128:(pt + 1) * 128,
                                                      ph * 512:(ph + 1) * 512],
                                              in_=youts[:, (u - 1) % 2, :])
                        p.inc(i_, f"sYW{(u - 1) % 2}", 16, f"yw_{u-1}")
            p.wait(gpsimd, "c3u_31")
            i_ = gpsimd.dma_start(out=y_d[15 * 128:16 * 128, 512:1024],
                                  in_=youts[:, 31 % 2, :])
            p.inc(i_, "sYW1", 16, "yw_31")
            for s2 in ["sPW0", "sPW1", "sYW0", "sYW1"]:
                p.marks[f"fin_{s2}"] = (s2, p.counts[s2])

    return nc


def _emit_c3(V, vector, p, g, psS, osc4, youts, xpin, dbg):
    for h in range(2):
        for j in range(4):
            t = 4 * g + j
            u = g * 8 + h * 4 + j
            V.wait(f"c2mmj_{g}_{h}_{j}")
            V.wait(f"cxp2_{u}")
            if u >= 2:
                V.wait(f"yw_{u-2}")
            V.op(V.e.tensor_scalar(out=youts[:, u % 2, :],
                                   in0=psS[:, j * 512:(j + 1) * 512],
                                   scalar1=osc4[:, j:j + 1], scalar2=None,
                                   op0=ALU.mult))
            V.op(V.e.tensor_add(youts[:, u % 2, :], youts[:, u % 2, :],
                                xpin[:, u % 2, :]), f"c3_{t}_{h}")
            p.marks[f"c3u_{u}"] = p.marks[f"c3_{t}_{h}"]


def _emit_qkv_ev(V, vector, p, ev, qf, kf, vtok, bcols, bvbc, psS):
    kind, a = ev
    if kind in ("q", "k"):
        dst = qf if kind == "q" else kf
        jb = 0 if kind == "q" else 1
        V.wait(f"qk_mm_{kind}_{a}")
        if kind == "q" and a == 0:
            V.wait("a0_done")
        V.op(vector.tensor_scalar(out=dst[:, a, :], in0=psS[:, :],
                                  scalar1=bcols[:, jb, a:a + 1],
                                  scalar2=SCL, op0=ALU.add, op1=ALU.mult),
             f"qk_ev_{kind}_{a}")
    else:
        V.wait(f"v_mm_{a}")
        V.op(vector.scalar_tensor_tensor(out=vtok[:, a, :], in0=psS[:, 0:1024],
                                         scalar=1.0, in1=bvbc[:],
                                         op0=ALU.mult, op1=ALU.add), f"v_ev_{a}")


_PROGRAM_CACHE = {}


def _build_program():
    marks = {}
    build_nc(1, marks)
    return build_nc(2, marks)


def kernel(L0_attn_x, L0_attn_Wq, L0_attn_bq, L0_attn_Wk, L0_attn_bk,
           L0_attn_Wv, L0_attn_bv, L0_attn_Wo, L0_attn_bo,
           L0_ffn_w_gate, L0_ffn_w_up, L0_ffn_w_down):
    if "nc" not in _PROGRAM_CACHE:
        _PROGRAM_CACHE["nc"] = _build_program()
    nc = _PROGRAM_CACHE["nc"]
    x = np.ascontiguousarray(np.asarray(L0_attn_x, dtype=np.float32))

    def tr(a):
        return np.ascontiguousarray(np.asarray(a, np.float32).T)

    common = {
        "wq": tr(L0_attn_Wq), "wk": tr(L0_attn_Wk),
        "wv": tr(L0_attn_Wv), "wo": tr(L0_attn_Wo),
        "bq": np.ascontiguousarray(np.asarray(L0_attn_bq, np.float32)),
        "bk": np.ascontiguousarray(np.asarray(L0_attn_bk, np.float32)),
        "bv": np.ascontiguousarray(np.asarray(L0_attn_bv, np.float32)),
        "bo": np.ascontiguousarray(np.asarray(L0_attn_bo, np.float32)),
        "wg": tr(L0_ffn_w_gate), "wu": tr(L0_ffn_w_up), "wd": tr(L0_ffn_w_down),
        "identb": np.eye(128, dtype=ml_dtypes.bfloat16),
    }
    in_maps = []
    for c in range(NCORES):
        m = dict(common)
        m["x"] = np.ascontiguousarray(x[c])
        in_maps.append(m)
    res = run_bass_kernel_spmd(nc, in_maps, core_ids=list(range(NCORES)))
    y = np.stack([np.asarray(r["y"]) for r in res.results], axis=0)
    if DEBUG:
        global DEBUG_OUT
        DEBUG_OUT = res.results
    return y
